# version 06
# speedup vs baseline: 1.2328x; 1.0194x over previous
import numpy as np
import ml_dtypes
import concourse.bass as bass
import concourse.mybir as mybir
from concourse.bass_utils import run_bass_kernel_spmd

F32 = mybir.dt.float32
BF16 = mybir.dt.bfloat16
AF = mybir.ActivationFunctionType
ALU = mybir.AluOpType
AX = mybir.AxisListType

NP_BF16 = ml_dtypes.bfloat16

D = 1024
NB = 2
NCORE = 8
EPS = 1e-5


class Ctx:

    def __init__(self, nc):
        self.nc = nc
        self.h = {}
        self.c = {}

    def sem(self, *names):
        for n in names:
            self.h[n] = self.nc.alloc_semaphore(n)
            self.c[n] = 0

    def inc(self, instr, name, n=1):
        instr.then_inc(self.h[name], n)
        self.c[name] += n
        return self.c[name]

    def wait(self, eng, name, val=None):
        eng.wait_ge(self.h[name], self.c[name] if val is None else val)

    def sb(self, name, shape, dtype):
        return self.nc.alloc_sbuf_tensor(name, shape, dtype).ap()

    def ps(self, name, shape, dtype=F32):
        return self.nc.alloc_psum_tensor(name, shape, dtype).ap()


def make_masks_np():
    i = np.arange(128)[:, None]
    j = np.arange(512)[None, :]
    ms = [(128 * mi + i <= j).astype(np.float32) for mi in range(4)]
    return np.concatenate(ms, axis=1).astype(NP_BF16)


CONSTS = {
    "c_ones_b": np.ones((128, 128), NP_BF16),
    "c_ones_f": np.ones((128, 128), np.float32),
    "c_ident_f": np.eye(128, dtype=np.float32),
    "c_ident_b": np.eye(128, dtype=NP_BF16),
    "c_dmask": make_masks_np(),
}


def declare_consts(nc):
    aps = {}
    for k, v in CONSTS.items():
        dt = BF16 if v.dtype == NP_BF16 else F32
        aps[k] = nc.dram_tensor(k, list(v.shape), dt, kind="ExternalInput").ap()
    return aps


def const_inputs():
    return dict(CONSTS)


def build_l0a(T, debug=False):
    NT = T // 128
    NJ = T // 512
    SCALE = 1.0 / 32.0

    nc = bass.Bass("TRN2", target_bir_lowering=False)
    x_in = nc.dram_tensor("x", [T, D], F32, kind="ExternalInput").ap()
    xT_in = nc.dram_tensor("xT", [D, T], F32, kind="ExternalInput").ap()
    wk_in = nc.dram_tensor("wk", [D, 128], F32, kind="ExternalInput").ap()
    wv_in = nc.dram_tensor("wv", [D, 128], F32, kind="ExternalInput").ap()
    wq_in = nc.dram_tensor("wq2", [D, 256], F32, kind="ExternalInput").ap()
    cst = declare_consts(nc)
    attn_out = nc.dram_tensor("attnT", [256, T], BF16, kind="ExternalOutput").ap()
    scr_m = nc.dram_tensor("scr_m", [NT, 128], F32, kind="Internal").ap()
    scr_r = nc.dram_tensor("scr_r", [NT, 128], F32, kind="Internal").ap()
    scr_mx = nc.dram_tensor("scr_mx", [3, 128], F32, kind="Internal").ap()
    if debug:
        dbg = {nm: nc.dram_tensor(nm, shp, dt, kind="ExternalOutput").ap()
               for nm, shp, dt in [("dbg_kt", [128, T], BF16), ("dbg_q1", [128, T], BF16),
                                   ("dbg_vtok", [128, T], BF16), ("dbg_mr", [2, T], F32)]}

    C = Ctx(nc)
    C.sem("lxa", "lxb", "lw", "gw", "gxa", "gxb", "dv", "pe", "ac", "so", "ls",
          "lma", "lmb")

    xstat = [C.sb(f"xstat{i}", [128, D], F32) for i in range(2)]
    xb = [C.sb(f"xb{i}", [128, 8 * 512], BF16) for i in range(2)]
    wkb = C.sb("wkb", [128, 8 * 128], BF16)
    wvb = C.sb("wvb", [128, 8 * 128], BF16)
    wqb = C.sb("wqb", [128, 8 * 256], BF16)
    onesb = C.sb("onesb", [128, 128], BF16)
    onesf = C.sb("onesf", [128, 128], F32)
    identf = C.sb("identf", [128, 128], F32)
    dmaskb = C.sb("dmaskb", [128, 2048], BF16)
    bnbuf = C.sb("bnbuf", [128, 12], F32)
    msv = C.sb("msv", [128, 2 * NT], F32)
    mcold = C.sb("mcold", [128, NT], F32)
    nmcol = C.sb("nmcol", [128, NT], F32)
    rcol = C.sb("rcol", [128, NT], F32)
    sqv = C.sb("sqv", [128, NT], F32)
    trm = C.sb("trm", [128, 128], F32)
    trr = C.sb("trr", [128, 128], F32)
    mrowb = [C.sb(f"mrowb{i}", [1, 512], F32) for i in range(2)]
    rrowb = [C.sb(f"rrowb{i}", [1, 512], F32) for i in range(2)]
    mb = C.sb("mb", [128, 512], F32)
    rb = C.sb("rb", [128, 512], F32)
    tmpf = C.sb("tmpf", [128, 512], F32)
    tmpv = C.sb("tmpv", [128, 128], F32)
    nws = C.sb("nws", [128, 4], F32)
    wsvrow = C.sb("wsvrow", [1, 128], F32)
    wsumVb = C.sb("wsumVb", [128, 128], F32)
    KT = C.sb("KT", [128, T], BF16)
    Q1T = C.sb("Q1T", [128, T], BF16)
    Q2T = C.sb("Q2T", [128, T], BF16)
    Vtok = C.sb("Vtok", [128, T], BF16)
    attnT = [C.sb(f"attnT{h}", [128, T], BF16) for h in range(2)]
    mx = C.sb("mx", [128, 4], F32)
    mxtmp = C.sb("mxtmp", [128, 1], F32)
    mxrow = C.sb("mxrow", [1, 3 * 128], F32)
    csc = C.sb("cscal", [1, 8], F32)
    negCb = [C.sb(f"negCb{h}", [128, 1], F32) for h in range(2)]
    PT = [C.sb(f"PTb{i}", [128, 1536], BF16) for i in range(2)]
    linv = C.sb("linv", [1, 512], F32)
    lb = C.sb("lb", [128, 512], F32)
    epsb = C.sb("epsb", [128, 1], F32)

    PS = C.ps("PS", [128, 4096], F32)
    SC = [PS[:, 0:1536], PS[:, 1536:3072]]
    PV = PS[:, 3072:3584]
    AUX = PS[:, 3584:4096]
    PK, PQ1, PQ2 = PS[:, 0:512], PS[:, 512:1024], PS[:, 1024:1536]
    PVt = PS[:, 1536:2048]
    PBM, PBR = PS[:, 2048:2560], PS[:, 2560:3072]
    PW = PS[:, 3072:3584]

    with nc.Block() as block:
        @block.sync
        def _(sync):
            for name, dst in [("c_ones_b", onesb), ("c_ones_f", onesf),
                              ("c_ident_f", identf), ("c_dmask", dmaskb)]:
                C.inc(sync.dma_start(out=dst[:, :], in_=cst[name][:, :]), "lw", 16)

        @block.gpsimd
        def _(g):
            C.inc(g.dma_start(out=wkb[:].rearrange("p (c d) -> p c d", d=128),
                              in_=wk_in.rearrange("(c p) d -> p c d", p=128)), "gw", 16)
            C.inc(g.dma_start(out=wvb[:].rearrange("p (c d) -> p c d", d=128),
                              in_=wv_in.rearrange("(c p) d -> p c d", p=128)), "gw", 16)
            C.inc(g.dma_start(out=wqb[:].rearrange("p (c d) -> p c d", d=256),
                              in_=wq_in.rearrange("(c p) d -> p c d", p=128)), "gw", 16)

        gx_w = C.c["gw"]

        @block.vector
        def _(v):
            C.inc(v.memset(epsb[:, :], EPS), "dv")
            C.inc(v.memset(mx[:, :], 0.0), "dv")

        dv0A = C.c["dv"]

        @block.sync
        def _(sync):
            for i in range(NT):
                if i >= 2:
                    sync.wait_ge(C.h["dv"], dv0A + 3 * (i - 1))
                C.inc(sync.dma_start(out=xstat[i % 2][:, :],
                                     in_=x_in[128 * i:128 * (i + 1), :]),
                      "lxa" if i % 2 == 0 else "lxb", 16)

        @block.vector
        def _(v):
            for i in range(NT):
                v.wait_ge(C.h["lxa" if i % 2 == 0 else "lxb"], 16 * (i // 2 + 1))
                if i > 0:
                    v.wait_ge(C.h["dv"], dv0A + 3 * i)
                C.inc(v.bn_stats(bnbuf[:, 0:6], xstat[i % 2][:, 0:512]), "dv")
                C.inc(v.bn_stats(bnbuf[:, 6:12], xstat[i % 2][:, 512:1024]), "dv")
                C.wait(v, "dv")
                C.inc(v.bn_aggr(msv[:, 2 * i:2 * i + 2], bnbuf[:, :]), "dv")
            C.wait(v, "dv")
            C.inc(v.tensor_copy(mcold[:, :], msv[:, 0::2]), "dv")
            C.inc(v.tensor_copy(sqv[:, :], msv[:, 1::2]), "dv")
            C.wait(v, "dv")
            C.inc(v.tensor_scalar_mul(nmcol[:, :], mcold[:, :], -1.0), "dv")

        dv_stats = C.c["dv"]

        @block.scalar
        def _(sc):
            sc.wait_ge(C.h["dv"], dv_stats)
            C.inc(sc.activation(sqv[:, :], sqv[:, :], AF.Sqrt, bias=epsb[:, 0:1]), "ac")

        ac_sqrt = C.c["ac"]

        @block.vector
        def _(v):
            v.wait_ge(C.h["ac"], ac_sqrt)
            C.inc(v.reciprocal(rcol[:, :], sqv[:, :]), "dv")

        dv_rcol = C.c["dv"]

        @block.tensor
        def _(t):
            t.wait_ge(C.h["lw"], 16 * 4)
            t.wait_ge(C.h["dv"], dv_rcol)
            C.inc(t.transpose(PS[0:NT, 0:128], mcold[:, 0:NT], identf[:, :]), "pe")
            C.inc(t.transpose(PS[0:NT, 512:640], rcol[:, 0:NT], identf[:, :]), "pe")

        pe_tr = C.c["pe"]

        @block.vector
        def _(v):
            v.wait_ge(C.h["pe"], pe_tr)
            C.inc(v.tensor_copy(trm[0:NT, 0:128], PS[0:NT, 0:128]), "dv")
            C.inc(v.tensor_copy(trr[0:NT, 0:128], PS[0:NT, 512:640]), "dv")

        dv_tr = C.c["dv"]

        @block.sync
        def _(sync):
            sync.wait_ge(C.h["dv"], dv_tr)
            C.inc(sync.dma_start(out=scr_m[:, :], in_=trm[0:NT, 0:128]), "ls", 16)
            C.inc(sync.dma_start(out=scr_r[:, :], in_=trr[0:NT, 0:128]), "ls", 16)
        ls_rows = C.c["ls"]
        mflat = scr_m.rearrange("a b -> (a b)")
        rflat = scr_r.rearrange("a b -> (a b)")

        @block.tensor
        def _(t):
            t.wait_ge(C.h["gw"], gx_w)
            t.wait_ge(C.h["lw"], 16 * 4)
            t.wait_ge(C.h["dv"], dv_tr)
            for ci in range(8):
                mm = t.matmul(PW[:, 0:1], wkb[:, 128 * ci:128 * (ci + 1)], onesb[:, 0:1],
                              start=(ci == 0), stop=(ci == 7))
            C.inc(mm, "pe")
            for ci in range(8):
                mm = t.matmul(PW[:, 1:2], wqb[:, 256 * ci:256 * ci + 128], onesb[:, 0:1],
                              start=(ci == 0), stop=(ci == 7))
            C.inc(mm, "pe")
            for ci in range(8):
                mm = t.matmul(PW[:, 2:3], wqb[:, 256 * ci + 128:256 * ci + 256], onesb[:, 0:1],
                              start=(ci == 0), stop=(ci == 7))
            C.inc(mm, "pe")
            for ci in range(8):
                mm = t.matmul(PW[:, 3:4], wvb[:, 128 * ci:128 * (ci + 1)], onesb[:, 0:1],
                              start=(ci == 0), stop=(ci == 7))
            C.inc(mm, "pe")

        pe_ws = C.c["pe"]

        @block.vector
        def _(v):
            v.wait_ge(C.h["pe"], pe_ws)
            C.inc(v.tensor_scalar_mul(nws[:, 0:3], PW[:, 0:3], -1.0), "dv")
            C.inc(v.tensor_copy(nws[:, 3:4], PW[:, 3:4]), "dv")

        dv_ws = C.c["dv"]

        @block.tensor
        def _(t):
            t.wait_ge(C.h["dv"], dv_ws)
            C.inc(t.transpose(PS[0:1, 3584:3712], nws[:, 3:4], identf[:, :]), "pe")

        pe_wsv = C.c["pe"]

        @block.vector
        def _(v):
            v.wait_ge(C.h["pe"], pe_wsv)
            C.inc(v.tensor_copy(wsvrow[0:1, 0:128], PS[0:1, 3584:3712]), "dv")

        dv_wsvr = C.c["dv"]

        @block.tensor
        def _(t):
            t.wait_ge(C.h["dv"], dv_wsvr)
            mm = t.matmul(PS[:, 3584:3712], onesf[0:1, 0:128], wsvrow[0:1, 0:128],
                          start=True, stop=True)
            C.inc(mm, "pe")

        pe_wsvb = C.c["pe"]

        @block.vector
        def _(v):
            v.wait_ge(C.h["pe"], pe_wsvb)
            C.inc(v.tensor_copy(wsumVb[:, :], PS[:, 3584:3712]), "dv")

        dv_b0 = C.c["dv"]
        pe_b0 = C.c["pe"]

        PB = 7
        PB = 6
        DB = 22

        @block.sync
        def _(sync):
            sync.wait_ge(C.h["ls"], ls_rows)
            for j in range(NJ):
                if j >= 2:
                    sync.wait_ge(C.h["pe"], pe_b0 + PB * (j - 2) + 2)
                C.inc(sync.dma_start(out=mrowb[j % 2][0:1, :],
                                     in_=mflat[None, 512 * j:512 * (j + 1)]),
                      "lma" if j % 2 == 0 else "lmb", 16)
                C.inc(sync.dma_start(out=rrowb[j % 2][0:1, :],
                                     in_=rflat[None, 512 * j:512 * (j + 1)]),
                      "lma" if j % 2 == 0 else "lmb", 16)

        @block.gpsimd
        def _(g):
            for j in range(NJ):
                if j >= 2:
                    g.wait_ge(C.h["pe"], pe_b0 + PB * (j - 1))
                C.inc(g.dma_start(out=xb[j % 2][:].rearrange("p (c t) -> p c t", t=512),
                                  in_=xT_in[:, 512 * j:512 * (j + 1)].rearrange("(c p) t -> p c t", p=128)),
                      "gxa" if j % 2 == 0 else "gxb", 16)

        @block.tensor
        def _(t):
            for j in range(NJ):
                t.wait_ge(C.h["gxa" if j % 2 == 0 else "gxb"], 16 * (j // 2 + 1))
                t.wait_ge(C.h["lma" if j % 2 == 0 else "lmb"], 32 * (j // 2 + 1))
                if j == 0:
                    t.wait_ge(C.h["dv"], dv_b0)
                else:
                    t.wait_ge(C.h["dv"], dv_b0 + DB * j)
                xbj = xb[j % 2]
                mm = t.matmul(PBM, onesf[0:1, 0:128], mrowb[j % 2][0:1, :],
                              start=True, stop=True)
                C.inc(mm, "pe")
                mm = t.matmul(PBR, onesf[0:1, 0:128], rrowb[j % 2][0:1, :],
                              start=True, stop=True)
                C.inc(mm, "pe")
                for ci in range(8):
                    mm = t.matmul(PK, wkb[:, 128 * ci:128 * (ci + 1)], xbj[:, 512 * ci:512 * (ci + 1)],
                                  start=(ci == 0), stop=(ci == 7))
                C.inc(mm, "pe")
                for ci in range(8):
                    mm = t.matmul(PQ1, wqb[:, 256 * ci:256 * ci + 128], xbj[:, 512 * ci:512 * (ci + 1)],
                                  start=(ci == 0), stop=(ci == 7))
                C.inc(mm, "pe")
                for ci in range(8):
                    mm = t.matmul(PQ2, wqb[:, 256 * ci + 128:256 * ci + 256], xbj[:, 512 * ci:512 * (ci + 1)],
                                  start=(ci == 0), stop=(ci == 7))
                C.inc(mm, "pe")
                for s in range(4):
                    for ci in range(8):
                        mm = t.matmul(PVt[:, 128 * s:128 * (s + 1)],
                                      xbj[:, 512 * ci + 128 * s:512 * ci + 128 * (s + 1)],
                                      wvb[:, 128 * ci:128 * (ci + 1)],
                                      start=(ci == 0), stop=(ci == 7))
                C.inc(mm, "pe")

        @block.vector
        def _(v):
            def step(op, pe_val=None):
                v.wait_ge(C.h["dv"], C.c["dv"])
                if pe_val is not None:
                    v.wait_ge(C.h["pe"], pe_val)
                C.inc(op(), "dv")

            for j in range(NJ):
                p0 = pe_b0 + PB * j
                sl = slice(512 * j, 512 * (j + 1))
                step(lambda: v.tensor_copy(mb[:, :], PBM), p0 + 1)
                step(lambda: v.tensor_copy(rb[:, :], PBR), p0 + 2)
                step(lambda: v.scalar_tensor_tensor(tmpf[:, :], mb[:, :], nws[:, 0:1], PK,
                                                    ALU.mult, ALU.add), p0 + 3)
                step(lambda: v.tensor_mul(KT[:, sl], tmpf[:, :], rb[:, :]))
                step(lambda: v.scalar_tensor_tensor(tmpf[:, :], mb[:, :], nws[:, 1:2], PQ1,
                                                    ALU.mult, ALU.add), p0 + 4)
                step(lambda: v.tensor_mul(Q1T[:, sl], tmpf[:, :], rb[:, :]))
                step(lambda: v.scalar_tensor_tensor(tmpf[:, :], mb[:, :], nws[:, 2:3], PQ2,
                                                    ALU.mult, ALU.add), p0 + 5)
                step(lambda: v.tensor_mul(Q2T[:, sl], tmpf[:, :], rb[:, :]))
                for s in range(4):
                    tc = 4 * j + s
                    step(lambda s=s: v.scalar_tensor_tensor(tmpv[:, :], wsumVb[:, :],
                                                            nmcol[:, tc:tc + 1],
                                                            PVt[:, 128 * s:128 * (s + 1)],
                                                            ALU.mult, ALU.add), p0 + 6)
                    step(lambda tc=tc: v.tensor_scalar_mul(Vtok[:, 128 * tc:128 * (tc + 1)],
                                                           tmpv[:, :], rcol[:, tc:tc + 1]))
                for col, srcb in [(0, KT), (1, Q1T), (2, Q2T)]:
                    step(lambda srcb=srcb: v.reduce_max(mxtmp[:, 0:1], srcb[:, sl], axis=AX.X,
                                                        apply_absolute_value=True))
                    step(lambda col=col: v.tensor_max(mx[:, col:col + 1], mx[:, col:col + 1],
                                                      mxtmp[:, 0:1]))

        dv_b = C.c["dv"]
        pe_b = C.c["pe"]
        assert dv_b == dv_b0 + DB * NJ, (dv_b, dv_b0, DB, NJ)
        assert pe_b == pe_b0 + PB * NJ

        @block.sync
        def _(sync):
            sync.wait_ge(C.h["dv"], dv_b)
            scr_mx_flat = scr_mx.rearrange("a b -> (a b)")
            for i in range(3):
                C.inc(sync.dma_start(out=scr_mx_flat[128 * i:128 * (i + 1), None],
                                     in_=mx[:, i:i + 1]), "ls", 16)
            C.wait(sync, "ls")
            C.inc(sync.dma_start(out=mxrow[0:1, :], in_=scr_mx.rearrange("a b -> (a b)")[None, :]), "ls", 16)

        ls_mx = C.c["ls"]

        @block.vector
        def _(v):
            v.wait_ge(C.h["ls"], ls_mx)
            C.inc(v.reduce_max(csc[0:1, 0:1], mxrow[0:1, 0:128], axis=AX.X), "dv")
            C.inc(v.reduce_max(csc[0:1, 1:2], mxrow[0:1, 128:256], axis=AX.X), "dv")
            C.inc(v.reduce_max(csc[0:1, 2:3], mxrow[0:1, 256:384], axis=AX.X), "dv")
            for h in range(2):
                v.wait_ge(C.h["dv"], C.c["dv"])
                C.inc(v.tensor_mul(csc[0:1, 3 + h:4 + h], csc[0:1, 0:1], csc[0:1, 1 + h:2 + h]), "dv")
                v.wait_ge(C.h["dv"], C.c["dv"])
                C.inc(v.tensor_scalar(csc[0:1, 3 + h:4 + h], csc[0:1, 3 + h:4 + h], 4.0, 80.0,
                                      ALU.mult, ALU.min), "dv")
                v.wait_ge(C.h["dv"], C.c["dv"])
                C.inc(v.tensor_scalar_mul(csc[0:1, 3 + h:4 + h], csc[0:1, 3 + h:4 + h], -1.0), "dv")

        dv_csc = C.c["dv"]

        @block.tensor
        def _(t):
            t.wait_ge(C.h["dv"], dv_csc)
            for h in range(2):
                mm = t.matmul(AUX[:, 0:1] if h == 0 else AUX[:, 1:2],
                              onesf[0:1, 0:128], csc[0:1, 3 + h:4 + h], start=True, stop=True)
            C.inc(mm, "pe")

        pe_cb = C.c["pe"]

        @block.vector
        def _(v):
            v.wait_ge(C.h["pe"], pe_cb)
            for h in range(2):
                C.inc(v.tensor_copy(negCb[h][:, 0:1], AUX[:, h:h + 1]), "dv")

        dv_c0 = C.c["dv"]
        pe_c0 = C.c["pe"]
        ac_c0 = C.c["ac"]

        def tile_plan(j):
            ncks = 4 * j + 4
            groups = []
            c0 = 0
            while c0 < ncks:
                gn = min(3, ncks - c0)
                groups.append((c0, gn))
                c0 += gn
            return ncks, groups

        pe_val = {}
        ac_val = {}
        dv_val = {}
        pe_cnt = pe_c0
        ac_cnt = ac_c0
        dv_cnt = dv_c0
        for h in range(2):
            for j in range(NJ):
                ncks, groups = tile_plan(j)
                G = len(groups)
                base_pe = pe_cnt
                pe_val[(h, j, "SC", 0)] = base_pe + 1
                for g in range(1, G):
                    pe_val[(h, j, "SC", g)] = base_pe + 2 * g
                    pe_val[(h, j, "PV", g - 1)] = base_pe + 2 * g + 1
                pe_val[(h, j, "PV", G - 1)] = base_pe + 2 * G
                pe_val[(h, j, "BC")] = base_pe + 2 * G + 1
                pe_cnt = base_pe + 2 * G + 1
                for g in range(G):
                    ac_cnt += 1
                    ac_val[(h, j, "EXP", g)] = ac_cnt
                nmask = 0
                for g, (c0, gn) in enumerate(groups):
                    for ck in range(c0, c0 + gn):
                        if ck >= 4 * j:
                            nmask += 1
                            dv_val[(h, j, "MSK", ck)] = dv_cnt + nmask
                dv_cnt += nmask
                dv_val[(h, j, "RECIP")] = dv_cnt + 1
                dv_val[(h, j, "LB")] = dv_cnt + 2
                dv_val[(h, j, "FIN")] = dv_cnt + 3
                dv_cnt += 3
                dv_val[(h, j, "MASKEND", )] = dv_cnt

        def masks_upto(h, j, g, groups):
            val = 0
            for gg in range(g + 1):
                c0, gn = groups[gg]
                for ck in range(c0, c0 + gn):
                    if ck >= 4 * j:
                        val = dv_val[(h, j, "MSK", ck)]
            return val

        @block.tensor
        def _(t):
            QTs = [Q1T, Q2T]
            for h in range(2):
                for j in range(NJ):
                    ncks, groups = tile_plan(j)
                    G = len(groups)
                    prev_fin = dv_val[(h, j - 1, "FIN")] if j > 0 else (
                        dv_val[(h - 1, NJ - 1, "FIN")] if h > 0 else dv_c0)
                    for g in range(G + 1):
                        if g < G:
                            c0, gn = groups[g]
                            if g >= 2:
                                t.wait_ge(C.h["ac"], ac_val[(h, j, "EXP", g - 2)])
                            elif (h, j) != (0, 0):
                                prev = (h, j - 1) if j > 0 else (h - 1, NJ - 1)
                                t.wait_ge(C.h["ac"], ac_val[(prev[0], prev[1], "EXP",
                                                             len(tile_plan(prev[1])[1]) - 1)])
                            for ck in range(c0, c0 + gn):
                                mm = t.matmul(SC[g % 2][:, 512 * (ck - c0):512 * (ck - c0 + 1)],
                                              KT[:, 128 * ck:128 * (ck + 1)],
                                              QTs[h][:, 512 * j:512 * (j + 1)],
                                              start=True, stop=True)
                            C.inc(mm, "pe")
                            assert C.c["pe"] == pe_val[(h, j, "SC", g)]
                        if g >= 1:
                            gm = g - 1
                            c0, gn = groups[gm]
                            t.wait_ge(C.h["ac"], ac_val[(h, j, "EXP", gm)])
                            mv = masks_upto(h, j, gm, groups)
                            if mv:
                                t.wait_ge(C.h["dv"], mv)
                            if gm == 0:
                                t.wait_ge(C.h["dv"], prev_fin)
                            for ck in range(c0, c0 + gn):
                                t.matmul(PV, Vtok[:, 128 * ck:128 * (ck + 1)],
                                         PT[gm % 2][:, 512 * (ck - c0):512 * (ck - c0 + 1)],
                                         start=(ck == 0), stop=(ck == ncks - 1))
                            for ck in range(c0, c0 + gn):
                                mm = t.matmul(AUX[0:1, 0:512], onesb[:, 0:1],
                                              PT[gm % 2][:, 512 * (ck - c0):512 * (ck - c0 + 1)],
                                              start=(ck == 0), stop=(ck == ncks - 1))
                            C.inc(mm, "pe")
                            assert C.c["pe"] == pe_val[(h, j, "PV", gm)]
                    t.wait_ge(C.h["dv"], dv_val[(h, j, "RECIP")])
                    mm = t.matmul(AUX[:, 0:512], onesf[0:1, 0:128], linv[0:1, 0:512],
                                  start=True, stop=True)
                    C.inc(mm, "pe")
                    assert C.c["pe"] == pe_val[(h, j, "BC")]

        @block.scalar
        def _(sc):
            sc.wait_ge(C.h["dv"], dv_c0)
            for h in range(2):
                for j in range(NJ):
                    ncks, groups = tile_plan(j)
                    G = len(groups)
                    for g, (c0, gn) in enumerate(groups):
                        sc.wait_ge(C.h["pe"], pe_val[(h, j, "SC", g)])
                        if g >= 2:
                            sc.wait_ge(C.h["pe"], pe_val[(h, j, "PV", g - 2)])
                        elif (h, j) != (0, 0):
                            prev = (h, j - 1) if j > 0 else (h - 1, NJ - 1)
                            Gp = len(tile_plan(prev[1])[1])
                            sc.wait_ge(C.h["pe"], pe_val[(prev[0], prev[1], "PV", Gp - 1)])
                        C.inc(sc.activation(PT[g % 2][:, 0:512 * gn], SC[g % 2][:, 0:512 * gn],
                                            AF.Exp, bias=negCb[h][:, 0:1], scale=SCALE), "ac")
                        assert C.c["ac"] == ac_val[(h, j, "EXP", g)]

        @block.vector
        def _(v):
            v.wait_ge(C.h["lw"], 16 * 4)
            for h in range(2):
                for j in range(NJ):
                    ncks, groups = tile_plan(j)
                    G = len(groups)
                    for g, (c0, gn) in enumerate(groups):
                        for ck in range(c0, c0 + gn):
                            if ck >= 4 * j:
                                mi = ck - 4 * j
                                v.wait_ge(C.h["ac"], ac_val[(h, j, "EXP", g)])
                                sl = slice(512 * (ck - c0), 512 * (ck - c0 + 1))
                                C.inc(v.tensor_mul(PT[g % 2][:, sl], PT[g % 2][:, sl],
                                                   dmaskb[:, 512 * mi:512 * (mi + 1)]), "dv")
                                assert C.c["dv"] == dv_val[(h, j, "MSK", ck)]
                    v.wait_ge(C.h["pe"], pe_val[(h, j, "PV", G - 1)])
                    C.inc(v.reciprocal(linv[0:1, 0:512], AUX[0:1, 0:512]), "dv")
                    assert C.c["dv"] == dv_val[(h, j, "RECIP")]
                    v.wait_ge(C.h["pe"], pe_val[(h, j, "BC")])
                    C.inc(v.tensor_copy(lb[:, :], AUX[:, 0:512]), "dv")
                    assert C.c["dv"] == dv_val[(h, j, "LB")]
                    v.wait_ge(C.h["dv"], dv_val[(h, j, "LB")])
                    C.inc(v.tensor_mul(attnT[h][:, 512 * j:512 * (j + 1)], PV, lb[:, :]), "dv")
                    assert C.c["dv"] == dv_val[(h, j, "FIN")]

        dv_attn = C.c["dv"]

        @block.sync
        def _(sync):
            sync.wait_ge(C.h["dv"], dv_attn)
            C.inc(sync.dma_start(out=attn_out[0:128, :], in_=attnT[0][:, :]), "so", 16)
            C.inc(sync.dma_start(out=attn_out[128:256, :], in_=attnT[1][:, :]), "so", 16)
            if debug:
                C.inc(sync.dma_start(out=dbg["dbg_kt"][:, :], in_=KT[:, :]), "so", 16)
                C.inc(sync.dma_start(out=dbg["dbg_q1"][:, :], in_=Q1T[:, :]), "so", 16)
                C.inc(sync.dma_start(out=dbg["dbg_vtok"][:, :], in_=Vtok[:, :]), "so", 16)
                C.inc(sync.dma_start(out=dbg["dbg_mr"][0:1, :], in_=mflat[None, :]), "so", 16)
                C.inc(sync.dma_start(out=dbg["dbg_mr"][1:2, :], in_=rflat[None, :]), "so", 16)
            C.wait(sync, "so")

    return nc


def l0a_inputs(T, inputs):
    x0 = np.asarray(inputs["L0_attn_x"], np.float32)
    x0 = x0[:, :T, :]
    wq = np.asarray(inputs["L0_attn_Wq"], np.float32)
    wk = np.ascontiguousarray(np.asarray(inputs["L0_attn_Wk"], np.float32))
    wv = np.ascontiguousarray(np.asarray(inputs["L0_attn_Wv"], np.float32))
    cc = const_inputs()
    maps = []
    for c in range(NCORE):
        b, s = c // 4, c % 4
        wq2 = np.ascontiguousarray(np.concatenate([wq[2 * s], wq[2 * s + 1]], axis=1))
        m = {"x": np.ascontiguousarray(x0[b]),
             "xT": np.ascontiguousarray(x0[b].T),
             "wk": wk, "wv": wv, "wq2": wq2}
        m.update(cc)
        maps.append(m)
    return maps


def kernel(**inputs):
    T = int(np.asarray(inputs["L0_attn_x"]).shape[1])
    NSPL = 4
    TPC = T // NSPL
    cores = list(range(NCORE))

    res = run_bass_kernel_spmd(build_l0a(T), l0a_inputs(T, inputs), core_ids=cores).results
    attnT = [np.asarray(r["attnT"]) for r in res]

    res = run_bass_kernel_spmd(build_l0b(TPC), l0b_inputs(T, inputs, attnT, NSPL),
                               core_ids=cores).results
    x1 = [np.asarray(r["x1"]) for r in res]
    lnx1 = [np.asarray(r["lnx1"]) for r in res]
    zpart = [np.asarray(r["zpart"]) for r in res]

    res = run_bass_kernel_spmd(build_l0c(TPC, T, NSPL),
                               l0c_inputs(T, inputs, x1, lnx1, zpart, NSPL),
                               core_ids=cores).results
    x2 = [np.asarray(r["x2"]) for r in res]

    x2f = [np.ascontiguousarray(np.concatenate([x2[NSPL * b + s] for s in range(NSPL)], axis=0))
           for b in range(2)]
    TH = T // 2
    nc1a = build_l1a(TH)
    state = None
    oh = []
    for half in range(2):
        x2h = [np.ascontiguousarray(x2f[b][TH * half:TH * (half + 1)]) for b in range(2)]
        res = run_bass_kernel_spmd(nc1a, l1a_inputs(TH, inputs, x2h, state),
                                   core_ids=cores).results
        oh.append([np.asarray(r["otok"]) for r in res])
        state = [{"sinit": np.asarray(res[c]["sfin"]), "ctq": np.asarray(res[c]["ctqo"]),
                  "ctk": np.asarray(res[c]["ctko"]), "ctv": np.asarray(res[c]["ctvo"]),
                  "vtl": np.asarray(res[c]["vtlo"])} for c in cores]
    oT = [np.ascontiguousarray(np.concatenate([oh[0][c], oh[1][c]], axis=0).T) for c in cores]

    res = run_bass_kernel_spmd(build_l1b(TPC), l1b_inputs(T, inputs, oT, x2, NSPL),
                               core_ids=cores).results
    out = np.empty((2, T, D), np.float32)
    for c in range(NCORE):
        b, s = c // NSPL, c % NSPL
        out[b, TPC * s:TPC * (s + 1)] = np.asarray(res[c]["x4"], np.float32)
    return out


def build_l0b(TPC, debug=False):
    NTc = TPC // 128
    NJc = TPC // 512

    nc = bass.Bass("TRN2", target_bir_lowering=False)
    at_in = nc.dram_tensor("attnTa", [D, TPC], BF16, kind="ExternalInput").ap()
    x0_in = nc.dram_tensor("x0own", [TPC, D], F32, kind="ExternalInput").ap()
    wp_in = nc.dram_tensor("wp", [D, D], F32, kind="ExternalInput").ap()
    bp_in = nc.dram_tensor("bp", [1, D], F32, kind="ExternalInput").ap()
    gw1_in = nc.dram_tensor("gw1", [D, D], F32, kind="ExternalInput").ap()
    gb1_in = nc.dram_tensor("gb1", [D, 1], F32, kind="ExternalInput").ap()
    gw2_in = nc.dram_tensor("gw2", [8, D], F32, kind="ExternalInput").ap()
    cst = declare_consts(nc)
    x1_out = nc.dram_tensor("x1", [TPC, D], F32, kind="ExternalOutput").ap()
    lnx1_out = nc.dram_tensor("lnx1", [TPC, D], BF16, kind="ExternalOutput").ap()
    zp_out = nc.dram_tensor("zpart", [8, 1], F32, kind="ExternalOutput").ap()
    if debug:
        dbg_h = nc.dram_tensor("dbg_h", [128, 8 * 512], BF16, kind="ExternalOutput").ap()
        dbg_xt = nc.dram_tensor("dbg_xt", [128, 8 * 512], BF16, kind="ExternalOutput").ap()
        dbg_gw2 = nc.dram_tensor("dbg_gw2", [128, 64], BF16, kind="ExternalOutput").ap()
        dbg_zps = nc.dram_tensor("dbg_zps", [8, 512], F32, kind="ExternalOutput").ap()
    scr_gw1 = nc.dram_tensor("scr_gw1", [D, D], BF16, kind="Internal").ap()

    C = Ctx(nc)
    C.sem("lw", "gw", "dv", "pe", "ac", "lsa", "lsb", "lt", "lta", "ltb", "sx1", "sln", "so")

    atb = C.sb("atb", [128, 8 * TPC], BF16)
    wpb = C.sb("wpb", [128, 8 * D], BF16)
    bpb = C.sb("bpb", [1, D], F32)
    bpbb = C.sb("bpbb", [128, D], F32)
    gb1b = C.sb("gb1b", [128, 8], F32)
    gw1tb = C.sb("gw1tb", [128, 8 * D], BF16)
    gw2tb = C.sb("gw2tb", [128, 8 * 8], BF16)
    x0t = [C.sb(f"x0t{i}", [128, D], F32) for i in range(2)]
    prj = C.sb("prj", [128, D], F32)
    x1t = C.sb("x1t", [128, D], F32)
    lnt = C.sb("lnt", [128, D], F32)
    lnb = C.sb("lnb", [128, D], BF16)
    bnbuf = C.sb("bnbuf", [128, 12], F32)
    stat = C.sb("stat", [128, 4], F32)
    epsb = C.sb("epsb", [128, 1], F32)
    hTb = C.sb("hTb", [128, 8 * 512], BF16)
    xT1b = [C.sb(f"xT1b{i}", [128, 8 * 512], BF16) for i in range(2)]
    zsum = C.sb("zsum", [8, 1], F32)
    ztmp = C.sb("ztmp", [8, 1], F32)
    onesf = C.sb("onesf", [128, 128], F32)
    zdbg = C.sb("zdbg", [128, 512], F32) if debug else None

    PS = C.ps("PS", [128, 4096], F32)

    with nc.Block() as block:
        @block.sync
        def _(sync):
            C.inc(sync.dma_start(out=onesf[:, :], in_=cst["c_ones_f"][:, :]), "lw", 16)
            C.inc(sync.dma_start(out=bpb[0:1, :], in_=bp_in[0:1, :]), "lw", 16)
            C.inc(sync.dma_start(out=atb[:].rearrange("p (c t) -> p c t", t=TPC),
                                 in_=at_in.rearrange("(c p) t -> p c t", p=128)), "lw", 16)

        @block.gpsimd
        def _(g):
            C.inc(g.dma_start(out=wpb[:].rearrange("p (c n) -> p c n", n=D),
                              in_=wp_in.rearrange("(c p) n -> p c n", p=128)), "gw", 16)
            C.inc(g.dma_start(out=scr_gw1[:, :], in_=gw1_in[:, :]), "gw", 16)
            for oc in range(8):
                C.inc(g.dma_start(out=gw2tb[:, 8 * oc:8 * (oc + 1)],
                                  in_=gw2_in[:, 128 * oc:128 * (oc + 1)].rearrange("e p -> p e"),
                                  allow_slow_non_contiguous=True), "gw", 16)
            C.inc(g.dma_start(out=gb1b[:, :],
                              in_=gb1_in.rearrange("(c p) x -> p (c x)", p=128),
                              allow_slow_non_contiguous=True), "gw", 16)

        gw_pre = C.c["gw"]

        @block.sync
        def _(sync):
            sync.wait_ge(C.h["gw"], gw_pre)
            for hc in range(8):
                C.inc(sync.dma_start(out=gw1tb[:, D * hc:D * (hc + 1)],
                                     in_=scr_gw1[:, 128 * hc:128 * (hc + 1)],
                                     transpose=True), "lt", 16)

        lt_all = C.c["lt"]

        @block.tensor
        def _(t):
            t.wait_ge(C.h["lw"], 16 * 3)
            for cc in range(2):
                mm = t.matmul(PS[:, 2048 + 512 * cc:2048 + 512 * (cc + 1)],
                              onesf[0:1, 0:128], bpb[0:1, 512 * cc:512 * (cc + 1)],
                              start=True, stop=True)
            C.inc(mm, "pe")

        pe_bp = C.c["pe"]

        @block.vector
        def _(v):
            C.inc(v.memset(epsb[:, :], EPS), "dv")
            C.inc(v.memset(zsum[:, :], 0.0), "dv")
            v.wait_ge(C.h["pe"], pe_bp)
            C.inc(v.tensor_copy(bpbb[:, :], PS[:, 2048:3072]), "dv")

        dv_pre = C.c["dv"]

        PBc, DBc, ABc = 2, 15, 2

        @block.sync
        def _(sync):
            for i in range(NTc + 1):
                if i < NTc:
                    if i >= 2:
                        sync.wait_ge(C.h["dv"], dv_pre + DBc * (i - 1))
                    C.inc(sync.dma_start(out=x0t[i % 2][:, :],
                                         in_=x0_in[128 * i:128 * (i + 1), :]),
                          "lsa" if i % 2 == 0 else "lsb", 16)
                if i >= 1:
                    sync.wait_ge(C.h["dv"], dv_pre + DBc * (i - 1) + 9)
                    C.inc(sync.dma_start(out=x1_out[128 * (i - 1):128 * i, :], in_=x1t[:, :]),
                          "sx1", 16)
                    sync.wait_ge(C.h["dv"], dv_pre + DBc * i)
                    C.inc(sync.dma_start(out=lnx1_out[128 * (i - 1):128 * i, :], in_=lnb[:, :]),
                          "sln", 16)

        @block.tensor
        def _(t):
            t.wait_ge(C.h["lw"], 16 * 3)
            t.wait_ge(C.h["gw"], gw_pre)
            for i in range(NTc):
                if i > 0:
                    t.wait_ge(C.h["dv"], dv_pre + DBc * (i - 1) + 2)
                for cc in range(2):
                    for k in range(8):
                        mm = t.matmul(PS[:, 512 * cc:512 * (cc + 1)],
                                      atb[:, k * TPC + 128 * i:k * TPC + 128 * (i + 1)],
                                      wpb[:, k * D + 512 * cc:k * D + 512 * (cc + 1)],
                                      start=(k == 0), stop=(k == 7))
                    C.inc(mm, "pe")

        @block.scalar
        def _(sc):
            for i in range(NTc):
                d0 = dv_pre + DBc * i
                sc.wait_ge(C.h["dv"], d0 + 6)
                C.inc(sc.activation(stat[:, 3:4], stat[:, 1:2], AF.Sqrt, bias=epsb[:, 0:1]), "ac")
                sc.wait_ge(C.h["dv"], d0 + 13)
                C.inc(sc.activation(stat[:, 3:4], stat[:, 1:2], AF.Sqrt, bias=epsb[:, 0:1]), "ac")

        @block.vector
        def _(v):
            for i in range(NTc):
                a0 = ABc * i

                def step(op, **kw):
                    v.wait_ge(C.h["dv"], C.c["dv"])
                    for s_, val in kw.items():
                        v.wait_ge(C.h[s_], val)
                    C.inc(op(), "dv")

                step(lambda: v.tensor_add(prj[:, 0:512], PS[:, 0:512], bpbb[:, 0:512]),
                     pe=pe_bp + PBc * i + 1)
                step(lambda: v.tensor_add(prj[:, 512:1024], PS[:, 512:1024], bpbb[:, 512:1024]),
                     pe=pe_bp + PBc * i + 2)
                step(lambda: v.bn_stats(bnbuf[:, 0:6], prj[:, 0:512]))
                step(lambda: v.bn_stats(bnbuf[:, 6:12], prj[:, 512:1024]))
                step(lambda: v.bn_aggr(stat[:, 0:2], bnbuf[:, :]))
                step(lambda: v.tensor_scalar_mul(stat[:, 2:3], stat[:, 0:1], -1.0))
                step(lambda: v.reciprocal(stat[:, 3:4], stat[:, 3:4]), ac=a0 + 1)
                step(lambda: v.tensor_scalar(lnt[:, :], prj[:, :], stat[:, 2:3], stat[:, 3:4],
                                             ALU.add, ALU.mult))
                step(lambda i=i: v.tensor_add(x1t[:, :], x0t[i % 2][:, :], lnt[:, :]),
                     sx1=16 * i, **{("lsa" if i % 2 == 0 else "lsb"): 16 * (i // 2 + 1)})
                step(lambda: v.bn_stats(bnbuf[:, 0:6], x1t[:, 0:512]))
                step(lambda: v.bn_stats(bnbuf[:, 6:12], x1t[:, 512:1024]))
                step(lambda: v.bn_aggr(stat[:, 0:2], bnbuf[:, :]))
                step(lambda: v.tensor_scalar_mul(stat[:, 2:3], stat[:, 0:1], -1.0))
                step(lambda: v.reciprocal(stat[:, 3:4], stat[:, 3:4]), ac=a0 + 2)
                step(lambda: v.tensor_scalar(lnb[:, :], x1t[:, :], stat[:, 2:3], stat[:, 3:4],
                                             ALU.add, ALU.mult), sln=16 * i)

        dv_main = C.c["dv"]
        assert dv_main == dv_pre + DBc * NTc
        sln_all = C.c["sln"]
        sx1_all = C.c["sx1"]

        PBg, ABg, DBg = 9, 8, 2
        pe_g0 = C.c["pe"]
        ac_g0 = C.c["ac"]
        dv_g0 = C.c["dv"]

        @block.sync
        def _(sync):
            sync.wait_ge(C.h["sln"], sln_all)
            for tj in range(NJc):
                if tj >= 2:
                    sync.wait_ge(C.h["pe"], pe_g0 + PBg * (tj - 1))
                for hc in range(8):
                    C.inc(sync.dma_start(out=xT1b[tj % 2][:, 512 * hc:512 * (hc + 1)],
                                         in_=lnx1_out[512 * tj:512 * (tj + 1), 128 * hc:128 * (hc + 1)],
                                         transpose=True), "lta" if tj % 2 == 0 else "ltb", 16)

        @block.tensor
        def _(t):
            t.wait_ge(C.h["lt"], lt_all)
            t.wait_ge(C.h["dv"], dv_main)
            for tj in range(NJc):
                t.wait_ge(C.h["lta" if tj % 2 == 0 else "ltb"], 128 * (tj // 2 + 1))
                for oc in range(8):
                    if oc >= 2:
                        t.wait_ge(C.h["ac"], ac_g0 + ABg * tj + oc - 1)
                    elif tj > 0:
                        t.wait_ge(C.h["ac"], ac_g0 + ABg * tj)
                    for hc in range(8):
                        mm = t.matmul(PS[:, 512 * (oc % 2):512 * (oc % 2 + 1)],
                                      gw1tb[:, D * hc + 128 * oc:D * hc + 128 * (oc + 1)],
                                      xT1b[tj % 2][:, 512 * hc:512 * (hc + 1)],
                                      start=(hc == 0), stop=(hc == 7))
                    C.inc(mm, "pe")
                t.wait_ge(C.h["ac"], ac_g0 + ABg * (tj + 1))
                if tj > 0:
                    t.wait_ge(C.h["dv"], dv_g0 + DBg * tj)
                for oc in range(8):
                    mm = t.matmul(PS[0:8, 1024:1536], gw2tb[:, 8 * oc:8 * (oc + 1)],
                                  hTb[:, 512 * oc:512 * (oc + 1)],
                                  start=(oc == 0), stop=(oc == 7))
                C.inc(mm, "pe")

        @block.scalar
        def _(sc):
            sc.wait_ge(C.h["gw"], gw_pre)
            for tj in range(NJc):
                for oc in range(8):
                    sc.wait_ge(C.h["pe"], pe_g0 + PBg * tj + oc + 1)
                    if tj > 0 and oc == 0:
                        sc.wait_ge(C.h["pe"], pe_g0 + PBg * tj)
                    C.inc(sc.activation(hTb[:, 512 * oc:512 * (oc + 1)],
                                        PS[:, 512 * (oc % 2):512 * (oc % 2 + 1)],
                                        AF.Relu, bias=gb1b[:, oc:oc + 1]), "ac")

        @block.vector
        def _(v):
            for tj in range(NJc):
                v.wait_ge(C.h["dv"], C.c["dv"])
                v.wait_ge(C.h["pe"], pe_g0 + PBg * (tj + 1))
                C.inc(v.reduce_sum(ztmp[0:8, 0:1], PS[0:8, 1024:1536], axis=AX.X), "dv")
                v.wait_ge(C.h["dv"], C.c["dv"])
                C.inc(v.tensor_add(zsum[0:8, 0:1], zsum[0:8, 0:1], ztmp[0:8, 0:1]), "dv")
                if debug and tj == 0:
                    v.wait_ge(C.h["dv"], C.c["dv"])
                    C.inc(v.tensor_copy(zdbg[0:8, 0:512], PS[0:8, 1024:1536]), "dv")

        dv_gate = C.c["dv"]

        @block.sync
        def _(sync):
            sync.wait_ge(C.h["dv"], dv_gate)
            sync.wait_ge(C.h["sx1"], sx1_all)
            C.inc(sync.dma_start(out=zp_out[:, :], in_=zsum[0:8, 0:1]), "so", 16)
            if debug:
                C.inc(sync.dma_start(out=dbg_h[:, :], in_=hTb[:, :]), "so", 16)
                C.inc(sync.dma_start(out=dbg_xt[:, :], in_=xT1b[0][:, :]), "so", 16)
                C.inc(sync.dma_start(out=dbg_gw2[:, :], in_=gw2tb[:, :]), "so", 16)
                C.inc(sync.dma_start(out=dbg_zps[:, :], in_=zdbg[0:8, 0:512]), "so", 16)
            C.wait(sync, "so")

    return nc


def l0b_inputs(T, inputs, attnT_by_core, nsplit=4):
    TPC = T // nsplit
    x0 = np.asarray(inputs["L0_attn_x"], np.float32)[:, :T, :]
    wp = np.ascontiguousarray(np.asarray(inputs["L0_attn_Wp"], np.float32))
    bp = np.asarray(inputs["L0_attn_bp"], np.float32).reshape(1, D)
    gw1 = np.ascontiguousarray(np.asarray(inputs["L0_ffn_gate_w1"], np.float32))
    gb1 = np.asarray(inputs["L0_ffn_gate_b1"], np.float32).reshape(D, 1)
    gw2 = np.ascontiguousarray(np.asarray(inputs["L0_ffn_gate_w2"], np.float32))
    cc = const_inputs()
    maps = []
    for c in range(2 * nsplit):
        b, s = c // nsplit, c % nsplit
        at_full = np.concatenate([attnT_by_core[4 * b + k] for k in range(4)], axis=0)
        at_own = np.ascontiguousarray(at_full[:, TPC * s:TPC * (s + 1)])
        m = {"attnTa": at_own,
             "x0own": np.ascontiguousarray(x0[b, TPC * s:TPC * (s + 1)]),
             "wp": wp, "bp": bp, "gw1": gw1, "gb1": gb1, "gw2": gw2}
        m.update(cc)
        maps.append(m)
    return maps


def build_l0c(TPC, T, NSPL, debug=False):
    F = 4096
    NJc = TPC // 512

    nc = bass.Bass("TRN2", target_bir_lowering=False)
    x1_in = nc.dram_tensor("x1", [TPC, D], F32, kind="ExternalInput").ap()
    lnx1_in = nc.dram_tensor("lnx1", [TPC, D], BF16, kind="ExternalInput").ap()
    zp_in = nc.dram_tensor("zparts", [8, NSPL], F32, kind="ExternalInput").ap()
    gb2_in = nc.dram_tensor("gb2", [8, 1], F32, kind="ExternalInput").ap()
    w1_in = nc.dram_tensor("w1", [F, D], F32, kind="ExternalInput").ap()
    b1_in = nc.dram_tensor("b1col", [128, 32], F32, kind="ExternalInput").ap()
    w2_in = nc.dram_tensor("w2", [D, F], F32, kind="ExternalInput").ap()
    b2_in = nc.dram_tensor("b2row", [1, D], F32, kind="ExternalInput").ap()
    tw1_in = nc.dram_tensor("tw1", [8, F, D], F32, kind="ExternalInput").ap()
    tb1_in = nc.dram_tensor("tb1", [8, F], F32, kind="ExternalInput").ap()
    tw2_in = nc.dram_tensor("tw2", [8, D, F], F32, kind="ExternalInput").ap()
    tb2_in = nc.dram_tensor("tb2", [8, D], F32, kind="ExternalInput").ap()
    cst = declare_consts(nc)
    x2_out = nc.dram_tensor("x2", [TPC, D], F32, kind="ExternalOutput").ap()
    lnx2_out = nc.dram_tensor("lnx2", [TPC, D], BF16, kind="ExternalOutput").ap()
    scr_mw1 = nc.dram_tensor("scr_mw1", [F, D], BF16, kind="Internal").ap()
    scr_mw2 = nc.dram_tensor("scr_mw2", [D, F], BF16, kind="Internal").ap()

    C = Ctx(nc)
    C.sem("lw", "gw", "dv", "pe", "ac", "lsa", "lsb", "lta", "ltb",
          "mta", "mtb", "m2a", "m2b", "twa", "twb", "sm1a", "sm1b", "sm2a", "sm2b", "sx2", "so")

    onesf = C.sb("onesf", [128, 128], F32)
    identf = C.sb("identf", [128, 128], F32)
    identb = C.sb("identb", [128, 128], BF16)
    zpb = C.sb("zpb", [8, NSPL], F32)
    gb2b = C.sb("gb2b", [8, 1], F32)
    gcol = C.sb("gcol", [8, 1], F32)
    ghi8 = C.sb("ghi8", [8, 1], BF16)
    glo8 = C.sb("glo8", [8, 1], BF16)
    gloF8 = C.sb("gloF8", [8, 1], F32)
    grow = C.sb("grow", [1, 8], F32)
    gcolF = C.sb("gcolF", [128, 8], F32)
    gcolBf = C.sb("gcolBf", [128, 8], BF16)
    gloF = C.sb("gloF", [128, 8], F32)
    gIhi = C.sb("gIhi", [128, 8 * 128], BF16)
    gIlo = C.sb("gIlo", [128, 8 * 128], BF16)
    tb1b = C.sb("tb1b", [8, F], BF16)
    tb2b = C.sb("tb2b", [8, D], BF16)
    b1c = C.sb("b1c", [128, 32], F32)
    b2r = C.sb("b2r", [1, D], F32)
    mb1 = C.sb("mb1", [128, 32], F32)
    mb2row = C.sb("mb2row", [1, D], F32)
    mb2b = C.sb("mb2b", [128, D], F32)
    twbuf = [C.sb(f"twbuf{i}", [128, 8 * 1024], BF16) for i in range(2)]
    w1t = [C.sb(f"w1t{i}", [128, 1024], BF16) for i in range(2)]
    mwt = C.sb("mwt", [128, 1024], BF16)
    lnxT = [C.sb(f"lnxT{i}", [128, 8 * 512], BF16) for i in range(2)]
    m1buf = [C.sb(f"m1buf{i}", [128, 8 * 512], BF16) for i in range(2)]
    m2buf = [C.sb(f"m2buf{i}", [128, 512], BF16) for i in range(2)]
    uT = C.sb("uT", [128, 32 * 512], BF16)
    ffn = [C.sb(f"ffn{s}", [128, D], F32) for s in range(4)]
    x1tt = [C.sb(f"x1tt{i}", [128, 4 * D], F32) for i in range(2)]
    x2t = C.sb("x2t", [128, D], F32)
    lnv = C.sb("lnv", [128, D], F32)
    lnb = C.sb("lnb", [128, D], BF16)
    bnbuf = C.sb("bnbuf", [128, 12], F32)
    stat = C.sb("stat", [128, 4], F32)
    epsb = C.sb("epsb", [128, 1], F32)

    PS = C.ps("PS", [128, 4096], F32)

    with nc.Block() as block:
        @block.sync
        def _(sync):
            C.inc(sync.dma_start(out=onesf[:, :], in_=cst["c_ones_f"][:, :]), "lw", 16)
            C.inc(sync.dma_start(out=identf[:, :], in_=cst["c_ident_f"][:, :]), "lw", 16)
            C.inc(sync.dma_start(out=identb[:, :], in_=cst["c_ident_b"][:, :]), "lw", 16)
            C.inc(sync.dma_start(out=zpb[:, :], in_=zp_in[:, :]), "lw", 16)
            C.inc(sync.dma_start(out=gb2b[:, :], in_=gb2_in[:, :]), "lw", 16)
            C.inc(sync.dma_start(out=b1c[:, :], in_=b1_in[:, :]), "lw", 16)
            C.inc(sync.dma_start(out=b2r[0:1, :], in_=b2_in[0:1, :]), "lw", 16)

        lw_all = C.c["lw"]

        @block.gpsimd
        def _(g):
            C.inc(g.dma_start(out=tb1b[:, :], in_=tb1_in[:, :]), "gw", 16)
            C.inc(g.dma_start(out=tb2b[:, :], in_=tb2_in[:, :]), "gw", 16)

        gw_pre = C.c["gw"]

        @block.vector
        def _(v):
            C.inc(v.memset(epsb[:, :], EPS), "dv")
            v.wait_ge(C.h["lw"], lw_all)
            C.inc(v.reduce_sum(gcol[0:8, 0:1], zpb[0:8, 0:NSPL], axis=AX.X), "dv")
            v.wait_ge(C.h["dv"], C.c["dv"])
            C.inc(v.tensor_scalar_mul(gcol[0:8, 0:1], gcol[0:8, 0:1], 1.0 / T), "dv")
            v.wait_ge(C.h["dv"], C.c["dv"])
            C.inc(v.tensor_add(gcol[0:8, 0:1], gcol[0:8, 0:1], gb2b[0:8, 0:1]), "dv")
            v.wait_ge(C.h["dv"], C.c["dv"])
            C.inc(v.tensor_copy(ghi8[0:8, 0:1], gcol[0:8, 0:1]), "dv")
            v.wait_ge(C.h["dv"], C.c["dv"])
            C.inc(v.tensor_sub(gloF8[0:8, 0:1], gcol[0:8, 0:1], ghi8[0:8, 0:1]), "dv")
            v.wait_ge(C.h["dv"], C.c["dv"])
            C.inc(v.tensor_copy(glo8[0:8, 0:1], gloF8[0:8, 0:1]), "dv")

        dv_g1 = C.c["dv"]

        @block.tensor
        def _(t):
            t.wait_ge(C.h["lw"], lw_all)
            t.wait_ge(C.h["dv"], dv_g1)
            C.inc(t.transpose(PS[0:1, 0:8], gcol[0:8, 0:1], identf[0:8, 0:8]), "pe")

        pe_g1 = C.c["pe"]

        @block.vector
        def _(v):
            v.wait_ge(C.h["pe"], pe_g1)
            C.inc(v.tensor_copy(grow[0:1, 0:8], PS[0:1, 0:8]), "dv")

        dv_g2 = C.c["dv"]

        @block.tensor
        def _(t):
            t.wait_ge(C.h["dv"], dv_g2)
            mm = t.matmul(PS[:, 512:520], onesf[0:1, 0:128], grow[0:1, 0:8],
                          start=True, stop=True)
            C.inc(mm, "pe")

        pe_g2 = C.c["pe"]

        @block.vector
        def _(v):
            v.wait_ge(C.h["pe"], pe_g2)
            C.inc(v.tensor_copy(gcolF[:, :], PS[:, 512:520]), "dv")
            v.wait_ge(C.h["dv"], C.c["dv"])
            C.inc(v.tensor_copy(gcolBf[:, :], gcolF[:, :]), "dv")
            v.wait_ge(C.h["dv"], C.c["dv"])
            C.inc(v.tensor_sub(gloF[:, :], gcolF[:, :], gcolBf[:, :]), "dv")
            for e in range(8):
                v.wait_ge(C.h["dv"], C.c["dv"])
                C.inc(v.tensor_scalar_mul(gIhi[:, 128 * e:128 * (e + 1)], identb[:, :],
                                          gcolF[:, e:e + 1]), "dv")
                v.wait_ge(C.h["dv"], C.c["dv"])
                C.inc(v.tensor_scalar_mul(gIlo[:, 128 * e:128 * (e + 1)], identb[:, :],
                                          gloF[:, e:e + 1]), "dv")

        dv_diag = C.c["dv"]

        @block.tensor
        def _(t):
            t.wait_ge(C.h["gw"], gw_pre)
            t.wait_ge(C.h["dv"], dv_diag)
            for fc in range(32):
                t.matmul(PS[:, 1024 + fc:1024 + fc + 1], tb1b[:, 128 * fc:128 * (fc + 1)],
                         ghi8[0:8, 0:1], start=True, stop=False)
                mm = t.matmul(PS[:, 1024 + fc:1024 + fc + 1], tb1b[:, 128 * fc:128 * (fc + 1)],
                              glo8[0:8, 0:1], start=False, stop=True)
            C.inc(mm, "pe")
            for cc in range(2):
                t.matmul(PS[0:1, 1536 + 512 * cc:1536 + 512 * (cc + 1)],
                         ghi8[0:8, 0:1], tb2b[:, 512 * cc:512 * (cc + 1)],
                         start=True, stop=False)
                mm = t.matmul(PS[0:1, 1536 + 512 * cc:1536 + 512 * (cc + 1)],
                              glo8[0:8, 0:1], tb2b[:, 512 * cc:512 * (cc + 1)],
                              start=False, stop=True)
            C.inc(mm, "pe")

        pe_mb = C.c["pe"]

        @block.vector
        def _(v):
            v.wait_ge(C.h["pe"], pe_mb)
            C.inc(v.tensor_add(mb1[:, :], PS[:, 1024:1056], b1c[:, :]), "dv")
            v.wait_ge(C.h["dv"], C.c["dv"])
            C.inc(v.tensor_add(mb2row[0:1, :], PS[0:1, 1536:2560], b2r[0:1, :]), "dv")

        dv_mb = C.c["dv"]

        @block.tensor
        def _(t):
            t.wait_ge(C.h["dv"], dv_mb)
            for cc in range(2):
                mm = t.matmul(PS[:, 2560 + 512 * cc:2560 + 512 * (cc + 1)],
                              onesf[0:1, 0:128], mb2row[0:1, 512 * cc:512 * (cc + 1)],
                              start=True, stop=True)
            C.inc(mm, "pe")

        pe_mb2 = C.c["pe"]

        @block.vector
        def _(v):
            v.wait_ge(C.h["pe"], pe_mb2)
            C.inc(v.tensor_copy(mb2b[:, :], PS[:, 2560:3584]), "dv")

        dv_pre = C.c["dv"]
        pe_pre = C.c["pe"]

        @block.gpsimd
        def _(g):
            for fi in range(32):
                if fi >= 2:
                    g.wait_ge(C.h["pe"], pe_pre + 2 * (fi - 1))
                sa = "twa" if fi % 2 == 0 else "twb"
                C.inc(g.dma_start(out=twbuf[fi % 2][:].rearrange("p (e n) -> p e n", n=1024),
                                  in_=tw1_in[:, 128 * fi:128 * (fi + 1), :].rearrange("e p n -> p e n")),
                      sa, 16)
                C.inc(g.dma_start(out=w1t[fi % 2][:, :],
                                  in_=w1_in[128 * fi:128 * (fi + 1), :]), sa, 16)

        @block.tensor
        def _(t):
            for fi in range(32):
                sa = "twa" if fi % 2 == 0 else "twb"
                t.wait_ge(C.h[sa], 32 * (fi // 2 + 1))
                if fi > 0:
                    t.wait_ge(C.h["dv"], dv_pre + 2 * fi)
                for hh in range(2):
                    sl = PS[:, 512 * hh:512 * (hh + 1)]
                    t.matmul(sl, identb[:, :], w1t[fi % 2][:, 512 * hh:512 * (hh + 1)],
                             start=True, stop=False)
                    for e in range(8):
                        t.matmul(sl, gIhi[:, 128 * e:128 * (e + 1)],
                                 twbuf[fi % 2][:, 1024 * e + 512 * hh:1024 * e + 512 * (hh + 1)],
                                 start=False, stop=False)
                        mm = t.matmul(sl, gIlo[:, 128 * e:128 * (e + 1)],
                                      twbuf[fi % 2][:, 1024 * e + 512 * hh:1024 * e + 512 * (hh + 1)],
                                      start=False, stop=(e == 7))
                    C.inc(mm, "pe")

        @block.vector
        def _(v):
            for fi in range(32):
                for hh in range(2):
                    v.wait_ge(C.h["dv"], C.c["dv"])
                    v.wait_ge(C.h["pe"], pe_pre + 2 * fi + hh + 1)
                    if fi >= 1:
                        v.wait_ge(C.h["sm1a" if hh == 0 else "sm1b"], 16 * fi)
                    C.inc(v.tensor_copy(mwt[:, 512 * hh:512 * (hh + 1)],
                                        PS[:, 512 * hh:512 * (hh + 1)]), "dv")

        @block.sync
        def _(sync):
            for fi in range(32):
                for hh in range(2):
                    sync.wait_ge(C.h["dv"], dv_pre + 2 * fi + hh + 1)
                    C.inc(sync.dma_start(out=scr_mw1[128 * fi:128 * (fi + 1), 512 * hh:512 * (hh + 1)],
                                         in_=mwt[:, 512 * hh:512 * (hh + 1)]),
                          "sm1a" if hh == 0 else "sm1b", 16)

        dv_w1 = C.c["dv"]
        pe_w1 = C.c["pe"]

        @block.gpsimd
        def _(g):
            for it in range(32):
                hc, ng = it // 4, it % 4
                if it >= 2:
                    g.wait_ge(C.h["pe"], pe_w1 + 2 * (it - 1))
                else:
                    g.wait_ge(C.h["pe"], pe_w1)
                sa = "twa" if it % 2 == 0 else "twb"
                C.inc(g.dma_start(out=twbuf[it % 2][:].rearrange("p (e n) -> p e n", n=1024),
                                  in_=tw2_in[:, 128 * hc:128 * (hc + 1),
                                             1024 * ng:1024 * (ng + 1)].rearrange("e p n -> p e n")),
                      sa, 16)
                C.inc(g.dma_start(out=w1t[it % 2][:, :],
                                  in_=w2_in[128 * hc:128 * (hc + 1), 1024 * ng:1024 * (ng + 1)]),
                      sa, 16)

        @block.tensor
        def _(t):
            for it in range(32):
                sa = "twa" if it % 2 == 0 else "twb"
                t.wait_ge(C.h[sa], 32 * 16 + 32 * (it // 2 + 1))
                if it > 0:
                    t.wait_ge(C.h["dv"], dv_w1 + 2 * it)
                else:
                    t.wait_ge(C.h["dv"], dv_w1)
                for hh in range(2):
                    sl = PS[:, 512 * hh:512 * (hh + 1)]
                    t.matmul(sl, identb[:, :], w1t[it % 2][:, 512 * hh:512 * (hh + 1)],
                             start=True, stop=False)
                    for e in range(8):
                        t.matmul(sl, gIhi[:, 128 * e:128 * (e + 1)],
                                 twbuf[it % 2][:, 1024 * e + 512 * hh:1024 * e + 512 * (hh + 1)],
                                 start=False, stop=False)
                        mm = t.matmul(sl, gIlo[:, 128 * e:128 * (e + 1)],
                                      twbuf[it % 2][:, 1024 * e + 512 * hh:1024 * e + 512 * (hh + 1)],
                                      start=False, stop=(e == 7))
                    C.inc(mm, "pe")

        @block.vector
        def _(v):
            for it in range(32):
                for hh in range(2):
                    v.wait_ge(C.h["dv"], C.c["dv"])
                    v.wait_ge(C.h["pe"], pe_w1 + 2 * it + hh + 1)
                    if it >= 1:
                        v.wait_ge(C.h["sm2a" if hh == 0 else "sm2b"], 16 * it)
                    elif hh == 0:
                        v.wait_ge(C.h["sm1a"], 16 * 32)
                        v.wait_ge(C.h["sm1b"], 16 * 32)
                    C.inc(v.tensor_copy(mwt[:, 512 * hh:512 * (hh + 1)],
                                        PS[:, 512 * hh:512 * (hh + 1)]), "dv")

        @block.sync
        def _(sync):
            for it in range(32):
                hc, ng = it // 4, it % 4
                for hh in range(2):
                    sync.wait_ge(C.h["dv"], dv_w1 + 2 * it + hh + 1)
                    C.inc(sync.dma_start(
                        out=scr_mw2[128 * hc:128 * (hc + 1),
                                    1024 * ng + 512 * hh:1024 * ng + 512 * (hh + 1)],
                        in_=mwt[:, 512 * hh:512 * (hh + 1)]),
                        "sm2a" if hh == 0 else "sm2b", 16)

        dv_w2 = C.c["dv"]
        pe_w2 = C.c["pe"]

        pe_m0 = C.c["pe"]
        dv_m0 = C.c["dv"]
        ac_m0 = C.c["ac"]

        up_pe = {}
        dnfc_pe = {}
        gel_ac = {}
        sqrt_ac = {}
        evac_dv = {}
        negm1_dv = {}
        negm2_dv = {}
        x2add_dv = {}
        lnb_dv = {}
        pe_cnt, ac_cnt, dv_cnt = pe_m0, ac_m0, dv_m0
        for tj in range(NJc):
            for fg in range(8):
                for fc4 in range(4):
                    pe_cnt += 1
                    up_pe[(tj, fg, fc4)] = pe_cnt
            for hg in range(2):
                for fc in range(32):
                    pe_cnt += 1
                    dnfc_pe[(tj, hg, fc)] = pe_cnt
            for idx in range(32):
                ac_cnt += 1
                gel_ac[(tj, idx)] = ac_cnt
            for s_ in range(4):
                for k_ in range(2):
                    ac_cnt += 1
                    sqrt_ac[(tj, s_, k_)] = ac_cnt
            for hg in range(2):
                for s_ in range(4):
                    dv_cnt += 1
                    evac_dv[(tj, hg, s_)] = dv_cnt
            for s_ in range(4):
                dv_cnt += 4
                negm1_dv[(tj, s_)] = dv_cnt
                dv_cnt += 3
                x2add_dv[(tj, s_)] = dv_cnt
                dv_cnt += 4
                negm2_dv[(tj, s_)] = dv_cnt
                dv_cnt += 2
                lnb_dv[(tj, s_)] = dv_cnt

        def g_m1(tj, fg):
            return 8 * tj + fg

        def g_m2(tj, hg, fc):
            return fc + 32 * (hg + 2 * tj)

        @block.gpsimd
        def _(g):
            for tj in range(NJc):
                if tj >= 2:
                    g.wait_ge(C.h["dv"], x2add_dv[(tj - 2, 3)])
                C.inc(g.dma_start(out=x1tt[tj % 2][:].rearrange("p (s d) -> p s d", d=D),
                                  in_=x1_in[512 * tj:512 * (tj + 1), :].rearrange(
                                      "(s p) d -> p s d", p=128)),
                      "lsa" if tj % 2 == 0 else "lsb", 16)
                for s_ in range(4):
                    g.wait_ge(C.h["dv"], x2add_dv[(tj, s_)])
                    C.inc(g.dma_start(out=x2_out[512 * tj + 128 * s_:512 * tj + 128 * (s_ + 1), :],
                                      in_=x2t[:, :]), "sx2", 16)
                    g.wait_ge(C.h["dv"], lnb_dv[(tj, s_)])
                    C.inc(g.dma_start(out=lnx2_out[512 * tj + 128 * s_:512 * tj + 128 * (s_ + 1), :],
                                      in_=lnb[:, :]), "so", 16)

        def lnxt_load(eng, tj):
            if tj >= 2:
                eng.wait_ge(C.h["pe"], up_pe[(tj - 2, 7, 3)])
            for hc in range(8):
                C.inc(eng.dma_start(out=lnxT[tj % 2][:, 512 * hc:512 * (hc + 1)],
                                    in_=lnx1_in[512 * tj:512 * (tj + 1), 128 * hc:128 * (hc + 1)],
                                    transpose=True), "lta" if tj % 2 == 0 else "ltb", 16)

        @block.sync
        def _(sync):
            sync.wait_ge(C.h["sm1a"], 16 * 32)
            sync.wait_ge(C.h["sm1b"], 16 * 32)
            sync.wait_ge(C.h["sm2a"], 16 * 32)
            sync.wait_ge(C.h["sm2b"], 16 * 32)
            for tj in range(NJc):
                lnxt_load(sync, tj)
                for fg in range(8):
                    gi = g_m1(tj, fg)
                    if gi >= 2:
                        pt = (tj, fg - 2) if fg >= 2 else (tj - 1, fg + 6)
                        sync.wait_ge(C.h["pe"], up_pe[(pt[0], pt[1], 3)])
                    for hc in range(8):
                        C.inc(sync.dma_start(out=m1buf[gi % 2][:, 512 * hc:512 * (hc + 1)],
                                             in_=scr_mw1[512 * fg:512 * (fg + 1),
                                                         128 * hc:128 * (hc + 1)],
                                             transpose=True), "mta" if gi % 2 == 0 else "mtb", 16)
                for hg in range(2):
                    for fc in range(32):
                        gi = g_m2(tj, hg, fc)
                        if gi >= 2:
                            p_fc = fc - 2
                            p_hg, p_tj = hg, tj
                            if p_fc < 0:
                                p_fc += 32
                                p_hg -= 1
                                if p_hg < 0:
                                    p_hg += 2
                                    p_tj -= 1
                            sync.wait_ge(C.h["pe"], dnfc_pe[(p_tj, p_hg, p_fc)])
                        C.inc(sync.dma_start(out=m2buf[gi % 2][:, :],
                                             in_=scr_mw2[512 * hg:512 * (hg + 1),
                                                         128 * fc:128 * (fc + 1)],
                                             transpose=True), "m2a" if gi % 2 == 0 else "m2b", 16)

        @block.tensor
        def _(t):
            for tj in range(NJc):
                for fg in range(8):
                    gi = g_m1(tj, fg)
                    t.wait_ge(C.h["mta" if gi % 2 == 0 else "mtb"], 128 * (gi // 2 + 1))
                    if fg == 0:
                        t.wait_ge(C.h["lta" if tj % 2 == 0 else "ltb"], 128 * (tj // 2 + 1))
                        if tj == 0:
                            t.wait_ge(C.h["dv"], dv_w2)
                    for fc4 in range(4):
                        idx = 4 * fg + fc4
                        if idx >= 2:
                            t.wait_ge(C.h["ac"], gel_ac[(tj, idx - 2)])
                        elif tj > 0:
                            t.wait_ge(C.h["ac"], gel_ac[(tj - 1, 31)])
                        sl = PS[:, 512 * (idx % 2):512 * (idx % 2 + 1)]
                        for hc in range(8):
                            mm = t.matmul(sl,
                                          m1buf[gi % 2][:, 512 * hc + 128 * fc4:512 * hc + 128 * (fc4 + 1)],
                                          lnxT[tj % 2][:, 512 * hc:512 * (hc + 1)],
                                          start=(hc == 0), stop=(hc == 7))
                        C.inc(mm, "pe")
                        assert C.c["pe"] == up_pe[(tj, fg, fc4)]
                for hg in range(2):
                    for fc in range(32):
                        gi = g_m2(tj, hg, fc)
                        t.wait_ge(C.h["m2a" if gi % 2 == 0 else "m2b"], 16 * (gi // 2 + 1))
                        if fc == 0:
                            t.wait_ge(C.h["ac"], gel_ac[(tj, 31)])
                            if hg == 1:
                                t.wait_ge(C.h["dv"], evac_dv[(tj, 0, 3)])
                            elif tj > 0:
                                t.wait_ge(C.h["dv"], evac_dv[(tj - 1, 1, 3)])
                        for s_ in range(4):
                            mm = t.matmul(PS[:, 512 * (2 + s_):512 * (3 + s_)],
                                          uT[:, 512 * fc + 128 * s_:512 * fc + 128 * (s_ + 1)],
                                          m2buf[gi % 2][:, :],
                                          start=(fc == 0), stop=(fc == 31))
                        C.inc(mm, "pe")
                        assert C.c["pe"] == dnfc_pe[(tj, hg, fc)]

        @block.scalar
        def _(sc):
            sc.wait_ge(C.h["dv"], dv_pre)
            for tj in range(NJc):
                for idx in range(32):
                    sc.wait_ge(C.h["pe"], up_pe[(tj, idx // 4, idx % 4)])
                    if idx == 0 and tj > 0:
                        sc.wait_ge(C.h["pe"], dnfc_pe[(tj - 1, 1, 31)])
                    C.inc(sc.activation(uT[:, 512 * idx:512 * (idx + 1)],
                                        PS[:, 512 * (idx % 2):512 * (idx % 2 + 1)],
                                        AF.Gelu_apprx_tanh, bias=mb1[:, idx:idx + 1]), "ac")
                    assert C.c["ac"] == gel_ac[(tj, idx)]
                for s_ in range(4):
                    sc.wait_ge(C.h["dv"], negm1_dv[(tj, s_)])
                    C.inc(sc.activation(stat[:, 3:4], stat[:, 1:2], AF.Sqrt, bias=epsb[:, 0:1]), "ac")
                    assert C.c["ac"] == sqrt_ac[(tj, s_, 0)]
                    sc.wait_ge(C.h["dv"], negm2_dv[(tj, s_)])
                    C.inc(sc.activation(stat[:, 3:4], stat[:, 1:2], AF.Sqrt, bias=epsb[:, 0:1]), "ac")
                    assert C.c["ac"] == sqrt_ac[(tj, s_, 1)]

        @block.vector
        def _(v):
            def step(op, **kw):
                v.wait_ge(C.h["dv"], C.c["dv"])
                for s_, val in kw.items():
                    v.wait_ge(C.h[s_], val)
                C.inc(op(), "dv")

            nst = 0
            for tj in range(NJc):
                for hg in range(2):
                    for s_ in range(4):
                        step(lambda hg=hg, s_=s_: v.tensor_add(
                            ffn[s_][:, 512 * hg:512 * (hg + 1)],
                            PS[:, 512 * (2 + s_):512 * (3 + s_)],
                            mb2b[:, 512 * hg:512 * (hg + 1)]),
                            pe=dnfc_pe[(tj, hg, 31)])
                        assert C.c["dv"] == evac_dv[(tj, hg, s_)]
                for s_ in range(4):
                    step(lambda s_=s_: v.bn_stats(bnbuf[:, 0:6], ffn[s_][:, 0:512]))
                    step(lambda s_=s_: v.bn_stats(bnbuf[:, 6:12], ffn[s_][:, 512:1024]))
                    step(lambda: v.bn_aggr(stat[:, 0:2], bnbuf[:, :]))
                    step(lambda: v.tensor_scalar_mul(stat[:, 2:3], stat[:, 0:1], -1.0))
                    assert C.c["dv"] == negm1_dv[(tj, s_)]
                    step(lambda: v.reciprocal(stat[:, 3:4], stat[:, 3:4]), ac=sqrt_ac[(tj, s_, 0)])
                    step(lambda s_=s_: v.tensor_scalar(lnv[:, :], ffn[s_][:, :], stat[:, 2:3],
                                                       stat[:, 3:4], ALU.add, ALU.mult))
                    step(lambda tj=tj, s_=s_: v.tensor_add(
                        x2t[:, :], x1tt[tj % 2][:, D * s_:D * (s_ + 1)], lnv[:, :]),
                        sx2=16 * (4 * tj + s_),
                        **{("lsa" if tj % 2 == 0 else "lsb"): 16 * (tj // 2 + 1)})
                    assert C.c["dv"] == x2add_dv[(tj, s_)]
                    step(lambda: v.bn_stats(bnbuf[:, 0:6], x2t[:, 0:512]))
                    step(lambda: v.bn_stats(bnbuf[:, 6:12], x2t[:, 512:1024]))
                    step(lambda: v.bn_aggr(stat[:, 0:2], bnbuf[:, :]))
                    step(lambda: v.tensor_scalar_mul(stat[:, 2:3], stat[:, 0:1], -1.0))
                    assert C.c["dv"] == negm2_dv[(tj, s_)]
                    step(lambda: v.reciprocal(stat[:, 3:4], stat[:, 3:4]), ac=sqrt_ac[(tj, s_, 1)])
                    step(lambda tj=tj, s_=s_: v.tensor_scalar(lnb[:, :], x2t[:, :], stat[:, 2:3],
                                                              stat[:, 3:4], ALU.add, ALU.mult),
                         so=16 * (4 * tj + s_))
                    assert C.c["dv"] == lnb_dv[(tj, s_)]

        @block.sync
        def _(sync):
            C.wait(sync, "sx2")
            C.wait(sync, "so")

    return nc


def l0c_inputs(T, inputs, x1_by_core, lnx1_by_core, zpart_by_core, nsplit=4):
    TPC = T // nsplit
    F = 4096
    w1 = np.ascontiguousarray(np.asarray(inputs["L0_ffn_W1"], np.float32))
    b1 = np.asarray(inputs["L0_ffn_b1"], np.float32)
    b1col = np.ascontiguousarray(b1.reshape(32, 128).T)
    w2 = np.ascontiguousarray(np.asarray(inputs["L0_ffn_W2"], np.float32))
    b2row = np.asarray(inputs["L0_ffn_b2"], np.float32).reshape(1, D)
    tw1 = np.ascontiguousarray(np.asarray(inputs["L0_ffn_TW1"], np.float32))
    tb1 = np.ascontiguousarray(np.asarray(inputs["L0_ffn_TW1"], np.float32)) if False else \
        np.ascontiguousarray(np.asarray(inputs["L0_ffn_Tb1"], np.float32))
    tw2 = np.ascontiguousarray(np.asarray(inputs["L0_ffn_TW2"], np.float32))
    tb2 = np.ascontiguousarray(np.asarray(inputs["L0_ffn_Tb2"], np.float32))
    gb2 = np.asarray(inputs["L0_ffn_gate_b2"], np.float32).reshape(8, 1)
    cc = const_inputs()
    maps = []
    for c in range(2 * nsplit):
        b, s = c // nsplit, c % nsplit
        zp = np.concatenate([zpart_by_core[nsplit * b + k] for k in range(nsplit)], axis=1)
        m = {"x1": x1_by_core[c], "lnx1": lnx1_by_core[c], "zparts": np.ascontiguousarray(zp),
             "gb2": gb2, "w1": w1, "b1col": b1col, "w2": w2, "b2row": b2row,
             "tw1": tw1, "tb1": tb1, "tw2": tw2, "tb2": tb2}
        m.update(cc)
        maps.append(m)
    return maps


L1B_SIMPLE = set()


def build_l1b(TPC, debug=False):
    F = 4096
    NTc = TPC // 128
    NJc = TPC // 512

    nc = bass.Bass("TRN2", target_bir_lowering=False)
    o_in = nc.dram_tensor("oTa", [D, TPC], BF16, kind="ExternalInput").ap()
    x2_in = nc.dram_tensor("x2own", [TPC, D], F32, kind="ExternalInput").ap()
    wo_in = nc.dram_tensor("wo", [D, D], F32, kind="ExternalInput").ap()
    w1_in = nc.dram_tensor("w1w", [D, F], F32, kind="ExternalInput").ap()
    w2_in = nc.dram_tensor("w2w", [D, F], F32, kind="ExternalInput").ap()
    h1_in = nc.dram_tensor("h1w", [F, F], F32, kind="ExternalInput").ap()
    h2_in = nc.dram_tensor("h2w", [F, F], F32, kind="ExternalInput").ap()
    w3_in = nc.dram_tensor("w3w", [F, D], F32, kind="ExternalInput").ap()
    w1s_in = nc.dram_tensor("w1s", [128, 32], F32, kind="ExternalInput").ap()
    w2s_in = nc.dram_tensor("w2s", [128, 32], F32, kind="ExternalInput").ap()
    h1s_in = nc.dram_tensor("h1s", [128, 32], F32, kind="ExternalInput").ap()
    h2s_in = nc.dram_tensor("h2s", [128, 32], F32, kind="ExternalInput").ap()
    w3s_in = nc.dram_tensor("w3s", [1, D], F32, kind="ExternalInput").ap()
    cst = declare_consts(nc)
    x4_out = nc.dram_tensor("x4", [TPC, D], F32, kind="ExternalOutput").ap()
    scr_x3 = nc.dram_tensor("scr_x3", [TPC, D], F32, kind="Internal").ap()
    scr_ln3 = nc.dram_tensor("scr_ln3", [TPC, D], BF16, kind="Internal").ap()
    scr_w = {nm: nc.dram_tensor(f"scr_{nm}", list(shp), BF16, kind="Internal").ap()
             for nm, shp in [("w1", (D, F)), ("w2", (D, F)), ("h1", (F, F)),
                             ("h2", (F, F)), ("w3", (F, D))]}

    C = Ctx(nc)
    C.sem("lw", "gw", "dv", "pe", "ac", "lsa", "lsb", "lta", "ltb", "loa", "lob",
          "pza", "pzb", "pzc", "pzd", "pha", "phb", "phc", "phd",
          "pwa", "pwb", "pwc", "pwd", "sx3", "sl3", "sx4")

    otbt = [C.sb(f"otbt{i}", [128, 8 * 128], BF16) for i in range(2)]
    wob = C.sb("wob", [128, 8 * D], BF16)
    x2t = [C.sb(f"x2t{i}", [128, D], F32) for i in range(2)]
    x3t = C.sb("x3t", [128, D], F32)
    lnb = C.sb("lnb", [128, D], BF16)
    bnbuf = C.sb("bnbuf", [128, 12], F32)
    stat = C.sb("stat", [128, 4], F32)
    epsb = C.sb("epsb", [128, 1], F32)
    onesf = C.sb("onesf", [128, 128], F32)
    scol = {nm: C.sb(f"scol_{nm}", [128, 32], F32) for nm in ["w1", "w2", "h1", "h2"]}
    w3sb = C.sb("w3sb", [128, D], F32)
    lnxT = [C.sb("lnxT0", [128, 8 * 512], BF16)] * 2
    wz1 = [C.sb(f"wz1_{i}", [128, 8 * 256], BF16) for i in range(2)]
    wz2 = [C.sb(f"wz2_{i}", [128, 8 * 256], BF16) for i in range(2)]
    hp2 = [C.sb(f"hp2_{i}", [128, 8 * 512], BF16) for i in range(2)]
    wp3b = [C.sb(f"wp3b{i}", [128, 512], BF16) for i in range(2)]
    z1T = C.sb("z1T", [128, 32 * 512], BF16)
    h1T = C.sb("h1T", [128, 32 * 512], BF16)
    h2T = C.sb("h2T", [128, 32 * 512], BF16)
    s1b = C.sb("s1b", [128, 512], BF16)
    s2b = C.sb("s2b", [128, 512], BF16)
    ffn = [C.sb(f"ffn{s}", [128, D], F32) for s in range(4)]
    x3r = [C.sb(f"x3r{i}", [128, D], F32) for i in range(2)]
    lnv = C.sb("lnv", [128, D], F32)

    PS = C.ps("PS", [128, 4096], F32)

    with nc.Block() as block:
        @block.sync
        def _(sync):
            C.inc(sync.dma_start(out=onesf[:, :], in_=cst["c_ones_f"][:, :]), "lw", 16)
            for nm, src_ in [("w1", w1s_in), ("w2", w2s_in), ("h1", h1s_in), ("h2", h2s_in)]:
                C.inc(sync.dma_start(out=scol[nm][:, :], in_=src_[:, :]), "lw", 16)
            C.inc(sync.dma_start(out=lnv[0:1, 0:D], in_=w3s_in[0:1, :]), "lw", 16)

        lw_all = C.c["lw"]

        @block.gpsimd
        def _(g):
            C.inc(g.dma_start(out=wob[:].rearrange("p (c n) -> p c n", n=D),
                              in_=wo_in.rearrange("(c p) n -> p c n", p=128)), "gw", 16)
            for nm, src_ in [("w1", w1_in), ("w2", w2_in), ("h1", h1_in),
                             ("h2", h2_in), ("w3", w3_in)]:
                C.inc(g.dma_start(out=scr_w[nm][:, :], in_=src_[:, :]), "gw", 16)

        gw_pre = C.c["gw"]

        @block.tensor
        def _(t):
            t.wait_ge(C.h["lw"], lw_all)
            for cc in range(2):
                mm = t.matmul(PS[:, 3072 + 512 * cc:3072 + 512 * (cc + 1)],
                              onesf[0:1, 0:128], lnv[0:1, 512 * cc:512 * (cc + 1)],
                              start=True, stop=True)
            C.inc(mm, "pe")

        pe_w3s = C.c["pe"]

        @block.vector
        def _(v):
            C.inc(v.memset(epsb[:, :], EPS), "dv")
            v.wait_ge(C.h["pe"], pe_w3s)
            C.inc(v.tensor_copy(w3sb[:, :], PS[:, 3072:4096]), "dv")

        dv_pre = C.c["dv"]

        PBo, DBo, ABo = 2, 15, 2

        @block.sync
        def _(sync):
            for i in range(NTc + 1):
                if i < NTc:
                    if i >= 2:
                        sync.wait_ge(C.h["dv"], dv_pre + DBo * (i - 1))
                        sync.wait_ge(C.h["pe"], pe_w3s + PBo * (i - 1))
                    C.inc(sync.dma_start(out=otbt[i % 2][:].rearrange("p (c t) -> p c t", t=128),
                                         in_=o_in[:, 128 * i:128 * (i + 1)].rearrange(
                                             "(c p) t -> p c t", p=128)),
                          "loa" if i % 2 == 0 else "lob", 16)
                    C.inc(sync.dma_start(out=x2t[i % 2][:, :],
                                         in_=x2_in[128 * i:128 * (i + 1), :]),
                          "lsa" if i % 2 == 0 else "lsb", 16)
                if i >= 1:
                    sync.wait_ge(C.h["dv"], dv_pre + DBo * (i - 1) + 9)
                    C.inc(sync.dma_start(out=scr_x3[128 * (i - 1):128 * i, :], in_=x3t[:, :]),
                          "sx3", 16)
                    sync.wait_ge(C.h["dv"], dv_pre + DBo * i)
                    C.inc(sync.dma_start(out=scr_ln3[128 * (i - 1):128 * i, :], in_=lnb[:, :]),
                          "sl3", 16)

        @block.tensor
        def _(t):
            t.wait_ge(C.h["gw"], gw_pre)
            for i in range(NTc):
                t.wait_ge(C.h["loa" if i % 2 == 0 else "lob"], 16 * (i // 2 + 1))
                if i > 0:
                    t.wait_ge(C.h["dv"], dv_pre + DBo * (i - 1) + 2)
                for cc in range(2):
                    for k in range(8):
                        mm = t.matmul(PS[:, 512 * cc:512 * (cc + 1)],
                                      otbt[i % 2][:, 128 * k:128 * (k + 1)],
                                      wob[:, k * D + 512 * cc:k * D + 512 * (cc + 1)],
                                      start=(k == 0), stop=(k == 7))
                    C.inc(mm, "pe")

        @block.scalar
        def _(sc):
            for i in range(NTc):
                d0 = dv_pre + DBo * i
                sc.wait_ge(C.h["dv"], d0 + 6)
                C.inc(sc.activation(stat[:, 3:4], stat[:, 1:2], AF.Sqrt, bias=epsb[:, 0:1]), "ac")
                sc.wait_ge(C.h["dv"], d0 + 13)
                C.inc(sc.activation(stat[:, 3:4], stat[:, 1:2], AF.Sqrt, bias=epsb[:, 0:1]), "ac")

        @block.vector
        def _(v):
            for i in range(NTc):
                a0 = ABo * i

                def step(op, **kw):
                    v.wait_ge(C.h["dv"], C.c["dv"])
                    for s_, val in kw.items():
                        v.wait_ge(C.h[s_], val)
                    C.inc(op(), "dv")

                step(lambda: v.tensor_copy(ffn[0][:, 0:512], PS[:, 0:512]),
                     pe=PBo * i + 1 + pe_w3s)
                step(lambda: v.tensor_copy(ffn[0][:, 512:1024], PS[:, 512:1024]),
                     pe=PBo * i + 2 + pe_w3s)
                step(lambda: v.bn_stats(bnbuf[:, 0:6], ffn[0][:, 0:512]))
                step(lambda: v.bn_stats(bnbuf[:, 6:12], ffn[0][:, 512:1024]))
                step(lambda: v.bn_aggr(stat[:, 0:2], bnbuf[:, :]))
                step(lambda: v.tensor_scalar_mul(stat[:, 2:3], stat[:, 0:1], -1.0))
                step(lambda: v.reciprocal(stat[:, 3:4], stat[:, 3:4]), ac=a0 + 1)
                step(lambda: v.tensor_scalar(lnv[:, :], ffn[0][:, :], stat[:, 2:3], stat[:, 3:4],
                                             ALU.add, ALU.mult))
                step(lambda i=i: v.tensor_add(x3t[:, :], x2t[i % 2][:, :], lnv[:, :]),
                     sx3=16 * i, **{("lsa" if i % 2 == 0 else "lsb"): 16 * (i // 2 + 1)})
                step(lambda: v.bn_stats(bnbuf[:, 0:6], x3t[:, 0:512]))
                step(lambda: v.bn_stats(bnbuf[:, 6:12], x3t[:, 512:1024]))
                step(lambda: v.bn_aggr(stat[:, 0:2], bnbuf[:, :]))
                step(lambda: v.tensor_scalar_mul(stat[:, 2:3], stat[:, 0:1], -1.0))
                step(lambda: v.reciprocal(stat[:, 3:4], stat[:, 3:4]), ac=a0 + 2)
                step(lambda: v.tensor_scalar(lnb[:, :], x3t[:, :], stat[:, 2:3], stat[:, 3:4],
                                             ALU.add, ALU.mult), sl3=16 * i)

        dv_wo = C.c["dv"]
        pe_wo = C.c["pe"]
        ac_wo = C.c["ac"]
        sx3_all = C.c["sx3"]
        sl3_all = C.c["sl3"]

        zpe, zac, zdv = {}, {}, {}
        hpe, hac = {}, {}
        wpe = {}
        evac_dv, negm_dv, x4_dv = {}, {}, {}
        pe_cnt, ac_cnt, dv_cnt = pe_wo, ac_wo, dv_wo
        for tj in range(NJc):
            for fg in range(16):
                for fc4 in range(2):
                    pe_cnt += 1
                    zpe[(tj, fg, fc4)] = pe_cnt
                    ac_cnt += 2
                    zac[(tj, 2 * fg + fc4)] = ac_cnt
                    dv_cnt += 1
                    zdv[(tj, 2 * fg + fc4)] = dv_cnt
            for lay in range(2):
                for fg2 in range(8):
                    for ic in range(32):
                        pe_cnt += 1
                        hpe[(tj, lay, fg2, ic)] = pe_cnt
                    ac_cnt += 4
                    hac[(tj, lay, fg2)] = ac_cnt
            for og in range(2):
                for ic in range(32):
                    pe_cnt += 1
                    wpe[(tj, og, ic)] = pe_cnt
            for og in range(2):
                for s_ in range(4):
                    dv_cnt += 1
                    evac_dv[(tj, og, s_)] = dv_cnt
            for s_ in range(4):
                dv_cnt += 4
                negm_dv[(tj, s_)] = dv_cnt
                dv_cnt += 3
                x4_dv[(tj, s_)] = dv_cnt
            for s_ in range(4):
                ac_cnt += 1

        sqac = {}
        ac_cnt2 = ac_wo
        for tj in range(NJc):
            ac_cnt2 += 32 * 2 + 2 * 8 * 4
            for s_ in range(4):
                ac_cnt2 += 1
                sqac[(tj, s_)] = ac_cnt2
        assert ac_cnt2 == ac_cnt

        @block.sync
        def _(sync):
            sync.wait_ge(C.h["sl3"], sl3_all)
            sync.wait_ge(C.h["sx3"], sx3_all)
            sync.wait_ge(C.h["gw"], gw_pre)
            for tj in range(NJc):
                if tj >= 1:
                    sync.wait_ge(C.h["pe"], zpe[(tj - 1, 15, 1)])
                for hc in range(8):
                    C.inc(sync.dma_start(out=lnxT[tj % 2][:, 512 * hc:512 * (hc + 1)],
                                         in_=scr_ln3[512 * tj:512 * (tj + 1), 128 * hc:128 * (hc + 1)],
                                         transpose=True), "lta" if tj % 2 == 0 else "ltb", 16)
                for fg in range(16):
                    gi = 16 * tj + fg
                    if gi >= 2:
                        pt = (tj, fg - 2) if fg >= 2 else (tj - 1, fg + 14)
                        sync.wait_ge(C.h["pe"], zpe[(pt[0], pt[1], 1)])
                    C.inc(sync.dma_start(
                        out=wz1[gi % 2][:].rearrange("p (c n) -> p c n", n=256),
                        in_=scr_w["w1"][:, 256 * fg:256 * (fg + 1)].rearrange(
                            "(c p) n -> p c n", p=128)), "pza" if gi % 2 == 0 else "pzc", 16)
                for lay in range(2):
                    nm = "h1" if lay == 0 else "h2"
                    for fg2 in range(8):
                        for qr in range(4):
                            gih = 4 * (fg2 + 8 * (lay + 2 * tj)) + qr
                            if gih % 2 != 0:
                                continue
                            if gih >= 2:
                                q, r = divmod(gih - 2, 4)
                                tjp, layp, fg2p = q // 16, (q // 8) % 2, q % 8
                                sync.wait_ge(C.h["pe"], hpe[(tjp, layp, fg2p, 8 * r + 7)])
                            ic0 = 8 * qr
                            C.inc(sync.dma_start(
                                out=hp2[gih % 2][:].rearrange(
                                    "p (c n) -> p c n", n=512),
                                in_=scr_w[nm][128 * ic0:128 * (ic0 + 8),
                                              512 * fg2:512 * (fg2 + 1)].rearrange(
                                    "(c p) n -> p c n", p=128)), "pha" if gih % 4 == 0 else "phc", 16)
                for og in range(2):
                    for ic in range(32):
                        gi3 = ic + 32 * (og + 2 * tj)
                        if gi3 % 2 != 0:
                            continue
                        if gi3 >= 2:
                            q = gi3 - 2
                            tjp, ogp, icp = q // 64, (q // 32) % 2, q % 32
                            sync.wait_ge(C.h["pe"], wpe[(tjp, ogp, icp)])
                        C.inc(sync.dma_start(
                            out=wp3b[gi3 % 2][:, :],
                            in_=scr_w["w3"][128 * ic:128 * (ic + 1),
                                            512 * og:512 * (og + 1)]), "pwa" if gi3 % 4 == 0 else "pwc", 16)
                for s_ in range(4):
                    u = 4 * tj + s_
                    if u >= 2:
                        pu = u - 2
                        sync.wait_ge(C.h["dv"], x4_dv[(pu // 4, pu % 4)])
                    C.inc(sync.dma_start(out=x3r[u % 2][:, :],
                                         in_=scr_x3[128 * u:128 * (u + 1), :]),
                          "lsa" if u % 2 == 0 else "lsb", 16)
                    sync.wait_ge(C.h["dv"], x4_dv[(tj, s_)])
                    C.inc(sync.dma_start(out=x4_out[512 * tj + 128 * s_:512 * tj + 128 * (s_ + 1), :],
                                         in_=lnv[:, :]), "sx4", 16)

        @block.gpsimd
        def _(g):
            g.wait_ge(C.h["gw"], gw_pre)
            g.wait_ge(C.h["sl3"], sl3_all)
            for tj in range(NJc):
                for fg in range(16):
                    gi = 16 * tj + fg
                    if gi >= 2:
                        pt = (tj, fg - 2) if fg >= 2 else (tj - 1, fg + 14)
                        g.wait_ge(C.h["pe"], zpe[(pt[0], pt[1], 1)])
                    C.inc(g.dma_start(
                        out=wz2[gi % 2][:].rearrange("p (c n) -> p c n", n=256),
                        in_=scr_w["w2"][:, 256 * fg:256 * (fg + 1)].rearrange(
                            "(c p) n -> p c n", p=128)), "pzb" if gi % 2 == 0 else "pzd", 16)
                for lay in range(2):
                    nm = "h1" if lay == 0 else "h2"
                    for fg2 in range(8):
                        for qr in range(4):
                            gih = 4 * (fg2 + 8 * (lay + 2 * tj)) + qr
                            if gih % 2 != 1:
                                continue
                            if gih >= 2:
                                q, r = divmod(gih - 2, 4)
                                tjp, layp, fg2p = q // 16, (q // 8) % 2, q % 8
                                g.wait_ge(C.h["pe"], hpe[(tjp, layp, fg2p, 8 * r + 7)])
                            ic0 = 8 * qr
                            C.inc(g.dma_start(
                                out=hp2[gih % 2][:].rearrange(
                                    "p (c n) -> p c n", n=512),
                                in_=scr_w[nm][128 * ic0:128 * (ic0 + 8),
                                              512 * fg2:512 * (fg2 + 1)].rearrange(
                                    "(c p) n -> p c n", p=128)), "phb" if gih % 4 == 1 else "phd", 16)
                for og in range(2):
                    for ic in range(32):
                        gi3 = ic + 32 * (og + 2 * tj)
                        if gi3 % 2 != 1:
                            continue
                        if gi3 >= 2:
                            q = gi3 - 2
                            tjp, ogp, icp = q // 64, (q // 32) % 2, q % 32
                            g.wait_ge(C.h["pe"], wpe[(tjp, ogp, icp)])
                        C.inc(g.dma_start(
                            out=wp3b[gi3 % 2][:, :],
                            in_=scr_w["w3"][128 * ic:128 * (ic + 1),
                                            512 * og:512 * (og + 1)]), "pwb" if gi3 % 4 == 1 else "pwd", 16)

        @block.tensor
        def _(t):
            for tj in range(NJc):
                for fg in range(16):
                    gi = 16 * tj + fg
                    t.wait_ge(C.h["pza" if gi % 2 == 0 else "pzc"], 16 * (gi // 2 + 1))
                    t.wait_ge(C.h["pzb" if gi % 2 == 0 else "pzd"], 16 * (gi // 2 + 1))
                    if fg == 0:
                        t.wait_ge(C.h["lta" if tj % 2 == 0 else "ltb"], 128 * (tj // 2 + 1))
                        if tj == 0:
                            t.wait_ge(C.h["dv"], dv_pre + DBo * NTc)
                        else:
                            t.wait_ge(C.h["dv"], evac_dv[(tj - 1, 1, 3)])
                    for fc4 in range(2):
                        idx = 2 * fg + fc4
                        bk = (idx % 2) * 2
                        if idx >= 2:
                            t.wait_ge(C.h["ac"], zac[(tj, idx - 2)])
                        elif tj > 0:
                            t.wait_ge(C.h["ac"], zac[(tj - 1, 31)])
                        for ic in range(8):
                            t.matmul(PS[:, 512 * bk:512 * (bk + 1)],
                                     wz1[gi % 2][:, 256 * ic + 128 * fc4:256 * ic + 128 * (fc4 + 1)],
                                     lnxT[tj % 2][:, 512 * ic:512 * (ic + 1)],
                                     start=(ic == 0), stop=(ic == 7))
                        for ic in range(8):
                            mm = t.matmul(PS[:, 512 * (bk + 1):512 * (bk + 2)],
                                          wz2[gi % 2][:, 256 * ic + 128 * fc4:256 * ic + 128 * (fc4 + 1)],
                                          lnxT[tj % 2][:, 512 * ic:512 * (ic + 1)],
                                          start=(ic == 0), stop=(ic == 7))
                        C.inc(mm, "pe")
                        assert C.c["pe"] == zpe[(tj, fg, fc4)]
                for lay in range(2):
                    srcT = z1T if lay == 0 else h1T
                    for fg2 in range(8):
                        for fc2 in range(4):
                            if fg2 >= 1 or lay >= 1 or tj >= 1:
                                pass
                        if (lay, fg2) != (0, 0):
                            p = (tj, lay, fg2 - 1) if fg2 >= 1 else (tj, lay - 1, 7)
                            t.wait_ge(C.h["ac"], hac[p])
                        else:
                            t.wait_ge(C.h["dv"], zdv[(tj, 31)])
                        for ic in range(32):
                            gih = 4 * (fg2 + 8 * (lay + 2 * tj)) + ic // 8
                            if ic % 8 == 0:
                                t.wait_ge(C.h[["pha", "phb", "phc", "phd"][gih % 4]],
                                          16 * (gih // 4 + 1))
                            for fc2 in range(4):
                                mm = t.matmul(PS[:, 512 * fc2:512 * (fc2 + 1)],
                                              hp2[gih % 2][:, 512 * (ic % 8) + 128 * fc2:
                                                           512 * (ic % 8) + 128 * (fc2 + 1)],
                                              srcT[:, 512 * ic:512 * (ic + 1)],
                                              start=(ic == 0), stop=(ic == 31))
                            C.inc(mm, "pe")
                            assert C.c["pe"] == hpe[(tj, lay, fg2, ic)]
                for og in range(2):
                    for ic in range(32):
                        gi3 = ic + 32 * (og + 2 * tj)
                        t.wait_ge(C.h[["pwa", "pwb", "pwc", "pwd"][gi3 % 4]],
                                  16 * (gi3 // 4 + 1))
                        if ic == 0:
                            t.wait_ge(C.h["ac"], hac[(tj, 1, 7)])
                            if og == 1:
                                t.wait_ge(C.h["dv"], evac_dv[(tj, 0, 3)])
                            elif tj > 0:
                                t.wait_ge(C.h["dv"], evac_dv[(tj - 1, 1, 3)])
                        for s_ in range(4):
                            mm = t.matmul(PS[:, 512 * s_:512 * (s_ + 1)],
                                          h2T[:, 512 * ic + 128 * s_:512 * ic + 128 * (s_ + 1)],
                                          wp3b[gi3 % 2][:, :],
                                          start=(ic == 0), stop=(ic == 31))
                        C.inc(mm, "pe")
                        assert C.c["pe"] == wpe[(tj, og, ic)]

        @block.scalar
        def _(sc):
            sc.wait_ge(C.h["lw"], lw_all)
            for tj in range(NJc):
                for idx in range(32):
                    bk = (idx % 2) * 2
                    sc.wait_ge(C.h["pe"], zpe[(tj, idx // 2, idx % 2)])
                    if idx > 0:
                        sc.wait_ge(C.h["dv"], zdv[(tj, idx - 1)])
                    elif tj > 0:
                        sc.wait_ge(C.h["dv"], zdv[(tj - 1, 31)])
                    if idx == 0 and tj > 0:
                        sc.wait_ge(C.h["pe"], hpe[(tj - 1, 0, 7, 31)])
                    C.inc(sc.activation(s1b[:, :], PS[:, 512 * bk:512 * (bk + 1)],
                                        AF.Silu, scale=scol["w1"][:, idx:idx + 1]), "ac")
                    C.inc(sc.activation(s2b[:, :], PS[:, 512 * (bk + 1):512 * (bk + 2)],
                                        AF.Copy, scale=scol["w2"][:, idx:idx + 1]), "ac")
                    assert C.c["ac"] == zac[(tj, idx)]
                for lay in range(2):
                    dstT = h1T if lay == 0 else h2T
                    for fg2 in range(8):
                        sc.wait_ge(C.h["pe"], hpe[(tj, lay, fg2, 31)])
                        if fg2 == 0:
                            if lay == 0 and tj > 0:
                                sc.wait_ge(C.h["pe"], hpe[(tj - 1, 1, 7, 31)])
                            elif lay == 1 and tj > 0:
                                sc.wait_ge(C.h["pe"], wpe[(tj - 1, 1, 31)])
                        nms = "h1" if lay == 0 else "h2"
                        for fc2 in range(4):
                            idx2 = 4 * fg2 + fc2
                            C.inc(sc.activation(dstT[:, 512 * idx2:512 * (idx2 + 1)],
                                                PS[:, 512 * fc2:512 * (fc2 + 1)],
                                                AF.Gelu, scale=scol[nms][:, idx2:idx2 + 1]), "ac")
                        assert C.c["ac"] == hac[(tj, lay, fg2)]
                for s_ in range(4):
                    sc.wait_ge(C.h["dv"], negm_dv[(tj, s_)])
                    C.inc(sc.activation(stat[:, 3:4], stat[:, 1:2], AF.Sqrt, bias=epsb[:, 0:1]), "ac")
                    assert C.c["ac"] == sqac[(tj, s_)]

        @block.vector
        def _(v):
            def step(op, **kw):
                v.wait_ge(C.h["dv"], C.c["dv"])
                for s_, val in kw.items():
                    v.wait_ge(C.h[s_], val)
                C.inc(op(), "dv")

            for tj in range(NJc):
                for idx in range(32):
                    step(lambda idx=idx: v.tensor_mul(z1T[:, 512 * idx:512 * (idx + 1)],
                                                      s1b[:, :], s2b[:, :]),
                         ac=zac[(tj, idx)])
                    assert C.c["dv"] == zdv[(tj, idx)]
                for og in range(2):
                    for s_ in range(4):
                        step(lambda og=og, s_=s_: v.scalar_tensor_tensor(
                            ffn[s_][:, 512 * og:512 * (og + 1)],
                            w3sb[:, 512 * og:512 * (og + 1)], 1.0,
                            PS[:, 512 * s_:512 * (s_ + 1)],
                            ALU.mult, ALU.mult) if False else v.tensor_mul(
                            ffn[s_][:, 512 * og:512 * (og + 1)],
                            PS[:, 512 * s_:512 * (s_ + 1)],
                            w3sb[:, 512 * og:512 * (og + 1)]),
                            pe=wpe[(tj, og, 31)])
                        assert C.c["dv"] == evac_dv[(tj, og, s_)]
                for s_ in range(4):
                    step(lambda s_=s_: v.bn_stats(bnbuf[:, 0:6], ffn[s_][:, 0:512]))
                    step(lambda s_=s_: v.bn_stats(bnbuf[:, 6:12], ffn[s_][:, 512:1024]))
                    step(lambda: v.bn_aggr(stat[:, 0:2], bnbuf[:, :]))
                    step(lambda: v.tensor_scalar_mul(stat[:, 2:3], stat[:, 0:1], -1.0))
                    assert C.c["dv"] == negm_dv[(tj, s_)]
                    step(lambda: v.reciprocal(stat[:, 3:4], stat[:, 3:4]), ac=sqac[(tj, s_)])
                    step(lambda s_=s_: v.tensor_scalar(lnv[:, :], ffn[s_][:, :], stat[:, 2:3],
                                                       stat[:, 3:4], ALU.add, ALU.mult),
                         sx4=16 * (4 * tj + s_))
                    step(lambda tj=tj, s_=s_: v.tensor_add(
                        lnv[:, :], x3r[(4 * tj + s_) % 2][:, :], lnv[:, :]),
                        **{("lsa" if (4 * tj + s_) % 2 == 0 else "lsb"):
                           16 * (NTc // 2) + 16 * ((4 * tj + s_) // 2 + 1)})
                    assert C.c["dv"] == x4_dv[(tj, s_)]

        @block.sync
        def _(sync):
            C.wait(sync, "sx4")

    return nc


def l1b_inputs(T, inputs, oT_by_core, x2_by_core, nsplit=4):
    TPC = T // nsplit
    F = 4096
    wo = np.ascontiguousarray(np.asarray(inputs["L1_attn_Wo"], np.float32))
    cc = const_inputs()

    def col32(v):
        return np.ascontiguousarray(np.asarray(v, np.float32).reshape(-1, 128).T)

    maps = []
    for c in range(2 * nsplit):
        b, s = c // nsplit, c % nsplit
        o_full = np.concatenate([oT_by_core[nsplit_l1a_core(b, h, nsplit)] for h in range(4)], axis=0)
        m = {"oTa": np.ascontiguousarray(o_full[:, TPC * s:TPC * (s + 1)]),
             "x2own": x2_by_core[c], "wo": wo,
             "w1w": np.ascontiguousarray(np.asarray(inputs["L1_ffn_w1_w"], np.float32).T),
             "w2w": np.ascontiguousarray(np.asarray(inputs["L1_ffn_w2_w"], np.float32).T),
             "h1w": np.ascontiguousarray(np.asarray(inputs["L1_ffn_h1_w"], np.float32).T),
             "h2w": np.ascontiguousarray(np.asarray(inputs["L1_ffn_h2_w"], np.float32).T),
             "w3w": np.ascontiguousarray(np.asarray(inputs["L1_ffn_w3_w"], np.float32).T),
             "w1s": col32(inputs["L1_ffn_w1_s"]), "w2s": col32(inputs["L1_ffn_w2_s"]),
             "h1s": col32(inputs["L1_ffn_h1_s"]), "h2s": col32(inputs["L1_ffn_h2_s"]),
             "w3s": np.asarray(inputs["L1_ffn_w3_s"], np.float32).reshape(1, D)}
        m.update(cc)
        maps.append(m)
    return maps


def nsplit_l1a_core(b, h, nsplit=4):
    return 4 * b + h if nsplit == 4 else 2 * b + (h % 2) if False else b * 4 + h


L1A_MAXPHASE = 99
L1A_SIMPLE = set()


def build_l1a(T, debug=False):
    NT = T // 128
    NJ = T // 512
    NG = T // 128
    NCH = T // 32
    DK = 256
    TP = T + 32

    nc = bass.Bass("TRN2", target_bir_lowering=False)
    x_in = nc.dram_tensor("x", [T, D], F32, kind="ExternalInput").ap()
    xT_in = nc.dram_tensor("xT", [D, T], F32, kind="ExternalInput").ap()
    wq_in = nc.dram_tensor("wq", [D, DK], F32, kind="ExternalInput").ap()
    wk_in = nc.dram_tensor("wk", [D, DK], F32, kind="ExternalInput").ap()
    wv_in = nc.dram_tensor("wv", [D, DK], F32, kind="ExternalInput").ap()
    cqd_in = nc.dram_tensor("cqd", [128, 8 * 128], F32, kind="ExternalInput").ap()
    ckd_in = nc.dram_tensor("ckd", [128, 8 * 128], F32, kind="ExternalInput").ap()
    cvd_in = nc.dram_tensor("cvd", [128, 8 * 128], F32, kind="ExternalInput").ap()
    fsd_in = nc.dram_tensor("fsd", [128, 6 * 128], F32, kind="ExternalInput").ap()
    fld_in = nc.dram_tensor("fld", [128, 62 * 128], F32, kind="ExternalInput").ap()
    smallw_in = nc.dram_tensor("smallw", [D, 8], F32, kind="ExternalInput").ap()
    bgs_in = nc.dram_tensor("bgs", [1, 1], F32, kind="ExternalInput").ap()
    bl_in = nc.dram_tensor("blrow", [1, 3], F32, kind="ExternalInput").ap()
    lt_in = nc.dram_tensor("ltemp", [1, 1], F32, kind="ExternalInput").ap()
    rms_in = nc.dram_tensor("rmsrow", [1, DK], F32, kind="ExternalInput").ap()
    mbL_in = nc.dram_tensor("mbLf", [128, 128], F32, kind="ExternalInput").ap()
    mbU_in = nc.dram_tensor("mbUf", [128, 128], F32, kind="ExternalInput").ap()
    mtu_in = nc.dram_tensor("mtu32", [32, 32], BF16, kind="ExternalInput").ap()
    cst = declare_consts(nc)
    sin_in = nc.dram_tensor("sinit", [128, 512], F32, kind="ExternalInput").ap()
    ctq_in = nc.dram_tensor("ctq", [128, 8], BF16, kind="ExternalInput").ap()
    ctk_in = nc.dram_tensor("ctk", [128, 8], BF16, kind="ExternalInput").ap()
    ctv_in = nc.dram_tensor("ctv", [128, 8], BF16, kind="ExternalInput").ap()
    vtl_in = nc.dram_tensor("vtl", [128, 64], BF16, kind="ExternalInput").ap()
    o_out = nc.dram_tensor("otok", [T, DK], BF16, kind="ExternalOutput").ap()
    sfin_out = nc.dram_tensor("sfin", [128, 512], F32, kind="ExternalOutput").ap()
    ctqo_out = nc.dram_tensor("ctqo", [128, 8], BF16, kind="ExternalOutput").ap()
    ctko_out = nc.dram_tensor("ctko", [128, 8], BF16, kind="ExternalOutput").ap()
    ctvo_out = nc.dram_tensor("ctvo", [128, 8], BF16, kind="ExternalOutput").ap()
    vtlo_out = nc.dram_tensor("vtlo", [128, 64], BF16, kind="ExternalOutput").ap()
    scr_m = nc.dram_tensor("scr_m", [NT, 128], F32, kind="Internal").ap()
    scr_r = nc.dram_tensor("scr_r", [NT, 128], F32, kind="Internal").ap()
    scr_bt = nc.dram_tensor("scr_bt", [NT, 128], F32, kind="Internal").ap()
    scr_uw = nc.dram_tensor("scr_uw", [NG * 128, 512], BF16, kind="Internal").ap()
    scr_knt = nc.dram_tensor("scr_knt", [NG * 128, DK], BF16, kind="Internal").ap()
    scr_wt = nc.dram_tensor("scr_wt", [NG * 128, DK], BF16, kind="Internal").ap()
    scr_del = nc.dram_tensor("scr_del", [T, DK], BF16, kind="Internal").ap()
    if debug:
        dbg = {nm: nc.dram_tensor(nm, shp, BF16, kind="ExternalOutput").ap()
               for nm, shp in [("dbg_q", [128, 2 * T]), ("dbg_k", [128, 2 * T]),
                               ("dbg_v", [128, 2 * TP]), ("dbg_uw", [NG * 128, 512]),
                               ("dbg_del", [T, DK])]}
        dbg_sm = nc.dram_tensor("dbg_sm", [128, 8 * 64], F32, kind="ExternalOutput").ap()
        dbg_ve = nc.dram_tensor("dbg_ve", [128, 2 * (T + 32)], BF16, kind="ExternalOutput").ap()
        dbg_w0 = nc.dram_tensor("dbg_w0", [128, 2 * 2 * 520], BF16, kind="ExternalOutput").ap()
        dbg_mr2 = nc.dram_tensor("dbg_mr2", [2, T], F32, kind="ExternalOutput").ap()
        dbg_lv = nc.dram_tensor("dbg_lv", [128, 6 * 128], F32, kind="ExternalOutput").ap()
        dbg_kv = nc.dram_tensor("dbg_kv", [128, 3 * 256], BF16, kind="ExternalOutput").ap()
        dbg_uw0 = nc.dram_tensor("dbg_uw0", [128, 512], BF16, kind="ExternalOutput").ap()
        dbg_ps = nc.dram_tensor("dbg_ps", [128, 512], F32, kind="ExternalOutput").ap()

    C = Ctx(nc)
    C.sem("lw", "gw", "lxa", "lxb", "gxa", "gxb", "dv", "pe", "ac", "ls",
          "sua", "sub", "ska", "skb", "swa", "swb", "lua", "lub", "lka", "lkb",
          "lwa", "lwb", "dla", "dlb", "sdla", "sdlb", "soo", "duw", "dac",
          "lma", "lmb", "lba", "lbb", "lct", "sst")

    xstat = [C.sb(f"xstat{i}", [128, D], F32) for i in range(2)]
    xb = [C.sb(f"xb{i}", [128, 8 * 512], BF16) for i in range(2)]
    wqb = C.sb("wqb", [128, 8 * DK], BF16)
    wkb = C.sb("wkb", [128, 8 * DK], BF16)
    wvb = C.sb("wvb", [128, 8 * DK], BF16)
    swb_ = C.sb("swb_", [128, 8 * 8], BF16)
    cdq = C.sb("cdq", [128, 8 * 128], BF16)
    cdk = C.sb("cdk", [128, 8 * 128], BF16)
    cdv = C.sb("cdv", [128, 8 * 128], BF16)
    fsb = C.sb("fsb", [128, 6 * 128], BF16)
    flb = C.sb("flb", [128, 62 * 128], BF16)
    onesb = C.sb("onesb", [128, 128], BF16)
    onesf = C.sb("onesf", [128, 128], F32)
    identf = C.sb("identf", [128, 128], F32)
    identb = C.sb("identb", [128, 128], BF16)
    mbLs = C.sb("mbLs", [128, 128], F32)
    mbUs = C.sb("mbUs", [128, 128], F32)
    mtu = C.sb("mtu", [32, 32], BF16)
    bnbuf = C.sb("bnbuf", [128, 12], F32)
    msv = C.sb("msv", [128, 2 * NT], F32)
    mcold = C.sb("mcold", [128, NT], F32)
    nmcol = C.sb("nmcol", [128, NT], F32)
    rcol = C.sb("rcol", [128, NT], F32)
    sqv = C.sb("sqv", [128, NT], F32)
    trm = C.sb("trm", [128, 128], F32)
    trr = C.sb("trr", [128, 128], F32)
    mrowb = [C.sb(f"mrowb{i}", [1, 512], F32) for i in range(2)]
    rrowb = [C.sb(f"rrowb{i}", [1, 512], F32) for i in range(2)]
    browb = [C.sb(f"browb{i}", [1, 128], F32) for i in range(2)]
    mb = C.sb("mb", [128, 512], F32)
    rb = C.sb("rb", [128, 512], F32)
    tmpf = C.sb("tmpf", [128, 512], F32)
    nws = C.sb("nws", [128, 6], F32)
    wsrow = C.sb("wsrow", [1, 8], F32)
    nws8 = C.sb("nws8", [128, 8], F32)
    qT = C.sb("qT", [128, 2 * T], BF16)
    kT = C.sb("kT", [128, 2 * T], BF16)
    vT = C.sb("vT", [128, 2 * TP], BF16)
    q0w = [C.sb(f"q0w{i}", [128, 2 * 520], BF16) for i in range(2)]
    k0w = [C.sb(f"k0w{i}", [128, 2 * 520], BF16) for i in range(2)]
    v0w = [C.sb(f"v0w{i}", [128, 2 * 520], BF16) for i in range(2)]
    smallout = C.sb("smallout", [128, 8 * NT], F32)
    wgcol = C.sb("wgcol", [128, NT], F32)
    wcols = C.sb("wcols", [128, 3 * NT], F32)
    betacol = C.sb("betacol", [128, NT], F32)
    s3buf = C.sb("s3buf", [128, 4], F32)
    bgb = C.sb("bgb", [128, 1], F32)
    blb = C.sb("blb", [128, 3], F32)
    invtb = C.sb("invtb", [128, 1], F32)
    rmsb = C.sb("rmsb", [128, DK], F32)
    epsb = C.sb("epsb", [128, 1], F32)
    eps6 = C.sb("eps6", [128, 1], F32)
    sqb = C.sb("sqb", [128, 256], BF16)
    rn2 = C.sb("rn2", [1, 256], F32)
    qrb = C.sb("qrb", [128, 128], F32)
    krb = C.sb("krb", [128, 128], F32)
    bb = C.sb("bb", [128, 128], F32)
    kbg = C.sb("kbg", [128, 256], BF16)
    vbg = C.sb("vbg", [128, 256], BF16)
    Lb = C.sb("Lb", [128, 128], F32)
    Ltb = C.sb("Ltb", [128, 128], F32)
    Lk = [C.sb(f"Lk{i}", [128, 128], F32) for i in range(2)]
    Lkt = [C.sb(f"Lkt{i}", [128, 128], F32) for i in range(4)]
    uwb = [C.sb(f"uwb{i}", [128, 512], BF16) for i in range(2)]
    uwf = [C.sb(f"uwf{i}", [128, 512], F32) for i in range(2)]
    kntb = [C.sb(f"kntb{i}", [128, DK], BF16) for i in range(2)]
    wtb = [C.sb(f"wtb{i}", [128, DK], BF16) for i in range(2)]
    Smir = C.sb("Smir", [128, 512], BF16)
    Sf32 = C.sb("Sf32", [128, 512], F32)
    upb = C.sb("upb", [32, 256], BF16)
    atb_ = C.sb("atb_", [32, 32], BF16)
    otmp = C.sb("otmp", [32, 256], BF16)
    otm2 = C.sb("otm2", [32, 256], BF16)
    uwl = [C.sb(f"uwl{i}", [128, 512], BF16) for i in range(2)]
    kntl = [C.sb(f"kntl{i}", [128, DK], BF16) for i in range(2)]
    wtl = [C.sb(f"wtl{i}", [128, DK], BF16) for i in range(2)]
    sTb = [C.sb(f"sTb{i}", [128, 1024], BF16) for i in range(2)]
    lTb = [C.sb(f"lTb{i}", [128, 1024], BF16) for i in range(2)]
    dtok = [C.sb(f"dtok{i}", [128, DK], BF16) for i in range(2)]
    vtok = C.sb("vtok", [128, DK], BF16)
    stok = C.sb("stok", [128, DK], BF16)
    ltok = C.sb("ltok", [128, DK], BF16)
    mixf = C.sb("mixf", [128, DK], F32)
    of_ = C.sb("of_", [128, DK], F32)
    osq = C.sb("osq", [128, DK], F32)
    orr = C.sb("orr", [128, 2], F32)
    obf = C.sb("obf", [128, DK], BF16)

    PSf = C.ps("PSf", [128, 3584], F32)
    PSb = C.ps("PSb", [128, 1024], BF16)
    SB0 = PSf[:, 2560:2816]
    SB1 = PSf[:, 3328:3584]

    with nc.Block() as block:
        @block.sync
        def _(sync):
            for name, dst in [("c_ones_b", onesb), ("c_ones_f", onesf),
                              ("c_ident_f", identf), ("c_ident_b", identb)]:
                C.inc(sync.dma_start(out=dst[:, :], in_=cst[name][:, :]), "lw", 16)
            C.inc(sync.dma_start(out=mbLs[:, :], in_=mbL_in[:, :]), "lw", 16)
            C.inc(sync.dma_start(out=mbUs[:, :], in_=mbU_in[:, :]), "lw", 16)
            C.inc(sync.dma_start(out=mtu[:, :], in_=mtu_in[:, :]), "lw", 16)
            C.inc(sync.dma_start(out=blb[0:1, 0:3], in_=bl_in[0:1, :]), "lw", 16)
            C.inc(sync.dma_start(out=invtb[0:1, 0:1], in_=lt_in[0:1, :]), "lw", 16)
            C.inc(sync.dma_start(out=bgb[0:1, 0:1], in_=bgs_in[0:1, :]), "lw", 16)
            C.inc(sync.dma_start(out=rmsb[0:1, 0:DK], in_=rms_in[0:1, :]), "lw", 16)
            C.inc(sync.dma_start(out=Sf32[:, :], in_=sin_in[:, :]), "lw", 16)

        lw_all = C.c["lw"]

        @block.gpsimd
        def _(g):
            for dst, src_, n in [(wqb, wq_in, DK), (wkb, wk_in, DK), (wvb, wv_in, DK),
                                 (swb_, smallw_in, 8)]:
                C.inc(g.dma_start(out=dst[:].rearrange("p (c d) -> p c d", d=n),
                                  in_=src_.rearrange("(c p) d -> p c d", p=128)), "gw", 16)
            for dst, src_ in [(cdq, cqd_in), (cdk, ckd_in), (cdv, cvd_in),
                              (fsb, fsd_in), (flb, fld_in)]:
                C.inc(g.dma_start(out=dst[:, :], in_=src_[:, :]), "gw", 16)

        gw_all = C.c["gw"]

        @block.vector
        def _(v):
            C.inc(v.memset(epsb[:, :], EPS), "dv")
            C.inc(v.memset(eps6[:, :], 1e-6), "dv")
            for i in range(2):
                for ch in range(2):
                    C.inc(v.memset(q0w[i][:, 520 * ch:520 * ch + 8], 0.0), "dv")
                    C.inc(v.memset(k0w[i][:, 520 * ch:520 * ch + 8], 0.0), "dv")
                    C.inc(v.memset(v0w[i][:, 520 * ch:520 * ch + 8], 0.0), "dv")
            for ch in range(2):
                C.inc(v.memset(vT[:, TP * ch:TP * ch + 32], 0.0), "dv")

        dv0A = C.c["dv"]

        @block.sync
        def _(sync):
            sync.wait_ge(C.h["dv"], dv0A)
            for w0, src_ in [(q0w, ctq_in), (k0w, ctk_in), (v0w, ctv_in)]:
                for mc in range(2):
                    C.inc(sync.dma_start(out=w0[0][:, 520 * mc + 4:520 * mc + 8],
                                         in_=src_[:, 4 * mc:4 * (mc + 1)]), "lct", 16)
            for ch in range(2):
                C.inc(sync.dma_start(out=vT[:, TP * ch:TP * ch + 32],
                                     in_=vtl_in[:, 32 * ch:32 * (ch + 1)]), "lct", 16)

        lct_all = C.c["lct"]

        @block.sync
        def _(sync):
            for i in range(NT):
                if i >= 2:
                    sync.wait_ge(C.h["dv"], dv0A + 3 * (i - 1))
                C.inc(sync.dma_start(out=xstat[i % 2][:, :],
                                     in_=x_in[128 * i:128 * (i + 1), :]),
                      "lxa" if i % 2 == 0 else "lxb", 16)

        @block.vector
        def _(v):
            for i in range(NT):
                v.wait_ge(C.h["lxa" if i % 2 == 0 else "lxb"], 16 * (i // 2 + 1))
                if i > 0:
                    v.wait_ge(C.h["dv"], dv0A + 3 * i)
                C.inc(v.bn_stats(bnbuf[:, 0:6], xstat[i % 2][:, 0:512]), "dv")
                C.inc(v.bn_stats(bnbuf[:, 6:12], xstat[i % 2][:, 512:1024]), "dv")
                C.wait(v, "dv")
                C.inc(v.bn_aggr(msv[:, 2 * i:2 * i + 2], bnbuf[:, :]), "dv")
            C.wait(v, "dv")
            C.inc(v.tensor_copy(mcold[:, :], msv[:, 0::2]), "dv")
            C.inc(v.tensor_copy(sqv[:, :], msv[:, 1::2]), "dv")
            C.wait(v, "dv")
            C.inc(v.tensor_scalar_mul(nmcol[:, :], mcold[:, :], -1.0), "dv")

        dv_stats = C.c["dv"]

        @block.scalar
        def _(sc):
            sc.wait_ge(C.h["dv"], dv_stats)
            C.inc(sc.activation(sqv[:, :], sqv[:, :], AF.Sqrt, bias=epsb[:, 0:1]), "ac")

        ac_sqrt = C.c["ac"]

        @block.vector
        def _(v):
            v.wait_ge(C.h["ac"], ac_sqrt)
            C.inc(v.reciprocal(rcol[:, :], sqv[:, :]), "dv")

        dv_rcol = C.c["dv"]

        @block.tensor
        def _(t):
            t.wait_ge(C.h["lw"], lw_all)
            t.wait_ge(C.h["dv"], dv_rcol)
            C.inc(t.transpose(PSf[0:NT, 0:128], mcold[:, 0:NT], identf[:, :]), "pe")
            C.inc(t.transpose(PSf[0:NT, 512:640], rcol[:, 0:NT], identf[:, :]), "pe")

        pe_tr = C.c["pe"]

        @block.vector
        def _(v):
            v.wait_ge(C.h["pe"], pe_tr)
            C.inc(v.tensor_copy(trm[0:NT, 0:128], PSf[0:NT, 0:128]), "dv")
            C.inc(v.tensor_copy(trr[0:NT, 0:128], PSf[0:NT, 512:640]), "dv")

        dv_tr = C.c["dv"]

        @block.sync
        def _(sync):
            sync.wait_ge(C.h["dv"], dv_tr)
            C.inc(sync.dma_start(out=scr_m[:, :], in_=trm[0:NT, 0:128]), "ls", 16)
            C.inc(sync.dma_start(out=scr_r[:, :], in_=trr[0:NT, 0:128]), "ls", 16)

        ls_rows = C.c["ls"]
        mflat = scr_m.rearrange("a b -> (a b)")
        rflat = scr_r.rearrange("a b -> (a b)")

        if L1A_MAXPHASE < 2:
            return nc
        @block.tensor
        def _(t):
            t.wait_ge(C.h["gw"], gw_all)
            t.wait_ge(C.h["dv"], dv_tr)
            for wi, wsb in enumerate([wqb, wkb, wvb]):
                for mc in range(2):
                    for ci in range(8):
                        mm = t.matmul(PSf[:, 1024 + 2 * wi + mc:1024 + 2 * wi + mc + 1],
                                      wsb[:, DK * ci + 128 * mc:DK * ci + 128 * (mc + 1)],
                                      onesb[:, 0:1], start=(ci == 0), stop=(ci == 7))
            C.inc(mm, "pe")
            for ci in range(8):
                mm = t.matmul(PSf[0:8, 1032:1033], swb_[:, 8 * ci:8 * (ci + 1)],
                              onesb[:, 0:1], start=(ci == 0), stop=(ci == 7))
            C.inc(mm, "pe")

        pe_ws = C.c["pe"]

        @block.vector
        def _(v):
            v.wait_ge(C.h["pe"], pe_ws)
            C.inc(v.tensor_scalar_mul(nws[:, 0:6], PSf[:, 1024:1030], -1.0), "dv")
            v.wait_ge(C.h["dv"], C.c["dv"])
            v.wait_ge(C.h["ls"], ls_rows)
            C.inc(v.tensor_copy(trm[0:8, 0:1], PSf[0:8, 1032:1033]), "dv")

        dv_ws = C.c["dv"]

        @block.tensor
        def _(t):
            t.wait_ge(C.h["dv"], dv_ws)
            C.inc(t.transpose(PSf[0:1, 1040:1048], trm[0:8, 0:1], identf[0:8, 0:8]), "pe")

        pe_wsr = C.c["pe"]

        @block.vector
        def _(v):
            v.wait_ge(C.h["pe"], pe_wsr)
            C.inc(v.tensor_copy(wsrow[0:1, 0:8], PSf[0:1, 1040:1048]), "dv")

        dv_wsr = C.c["dv"]

        @block.tensor
        def _(t):
            t.wait_ge(C.h["dv"], dv_wsr)
            mm = t.matmul(PSf[:, 1040:1048], onesf[0:1, 0:128], wsrow[0:1, 0:8],
                          start=True, stop=True)
            mm = t.matmul(PSf[:, 1048:1049], onesf[0:1, 0:128], bgb[0:1, 0:1], start=True, stop=True)
            mm = t.matmul(PSf[:, 1056:1059], onesf[0:1, 0:128], blb[0:1, 0:3], start=True, stop=True)
            mm = t.matmul(PSf[:, 1064:1065], onesf[0:1, 0:128], invtb[0:1, 0:1], start=True, stop=True)
            mm = t.matmul(PSf[:, 1536:1536 + DK], onesf[0:1, 0:128], rmsb[0:1, 0:DK],
                          start=True, stop=True)
            C.inc(mm, "pe")

        pe_bc = C.c["pe"]

        @block.scalar
        def _(sc):
            sc.wait_ge(C.h["pe"], pe_bc)
            C.inc(sc.activation(invtb[:, 0:1], PSf[:, 1064:1065], AF.Exp, scale=-1.0), "ac")

        ac_invt = C.c["ac"]

        @block.vector
        def _(v):
            v.wait_ge(C.h["pe"], pe_bc)
            C.inc(v.tensor_copy(nws8[:, :], PSf[:, 1040:1048]), "dv")
            v.wait_ge(C.h["dv"], C.c["dv"])
            C.inc(v.tensor_scalar_mul(nws8[:, :], nws8[:, :], -1.0), "dv")
            C.inc(v.tensor_copy(bgb[:, 0:1], PSf[:, 1048:1049]), "dv")
            C.inc(v.tensor_copy(blb[:, 0:3], PSf[:, 1056:1059]), "dv")
            C.inc(v.tensor_copy(rmsb[:, :], PSf[:, 1536:1536 + DK]), "dv")

        dv_b0 = C.c["dv"]
        pe_b0 = C.c["pe"]

        if L1A_MAXPHASE < 3:
            return nc
        PB1, DB1, AB1 = 18, 28, 6
        NJB = 1 if "b1tile" in L1A_SIMPLE else NJ

        @block.sync
        def _(sync):
            sync.wait_ge(C.h["ls"], ls_rows)
            for j in range(NJB):
                if j >= 2:
                    sync.wait_ge(C.h["pe"], pe_b0 + PB1 * (j - 2) + 2)
                C.inc(sync.dma_start(out=mrowb[j % 2][0:1, :],
                                     in_=mflat[None, 512 * j:512 * (j + 1)]),
                      "lma" if j % 2 == 0 else "lmb", 16)
                C.inc(sync.dma_start(out=rrowb[j % 2][0:1, :],
                                     in_=rflat[None, 512 * j:512 * (j + 1)]),
                      "lma" if j % 2 == 0 else "lmb", 16)

        @block.gpsimd
        def _(g):
            for j in range(NJB):
                if j >= 2:
                    g.wait_ge(C.h["pe"], pe_b0 + PB1 * (j - 1))
                C.inc(g.dma_start(out=xb[j % 2][:].rearrange("p (c t) -> p c t", t=512),
                                  in_=xT_in[:, 512 * j:512 * (j + 1)].rearrange("(c p) t -> p c t", p=128)),
                      "gxa" if j % 2 == 0 else "gxb", 16)

        if "bloadsonly" in L1A_SIMPLE:
            return nc
        @block.tensor
        def _(t):
            pr = {}
            for j in range(NJB):
                t.wait_ge(C.h["gxa" if j % 2 == 0 else "gxb"], 16 * (j // 2 + 1))
                t.wait_ge(C.h["lma" if j % 2 == 0 else "lmb"], 32 * (j // 2 + 1))
                if j == 0:
                    t.wait_ge(C.h["lct"], lct_all)
                    t.wait_ge(C.h["dv"], dv_b0)
                    t.wait_ge(C.h["ac"], ac_invt)
                else:
                    t.wait_ge(C.h["dv"], dv_b0 + DB1 * j)
                    t.wait_ge(C.h["ac"], ac_invt + AB1 * j)
                xbj = xb[j % 2]
                if "nobcast" in L1A_SIMPLE:
                    mm = t.matmul(PSf[:, 3072:3584], identb[:, :], xbj[:, 0:512],
                                  start=True, stop=True)
                    C.inc(mm, "pe")
                    t.wait_ge(C.h["dv"], dv_b0 + DB1 * j + 1)
                    mm = t.matmul(PSf[:, 3072:3584], identb[:, :], xbj[:, 0:512],
                                  start=True, stop=True)
                    C.inc(mm, "pe")
                else:
                    mm = t.matmul(PSf[:, 3072:3584], onesf[0:1, 0:128],
                                  mrowb[j % 2][0:1, :], start=True, stop=True)
                    C.inc(mm, "pe")
                    t.wait_ge(C.h["dv"], dv_b0 + DB1 * j + 1)
                    mm = t.matmul(PSf[:, 3072:3584], onesf[0:1, 0:128],
                                  rrowb[j % 2][0:1, :], start=True, stop=True)
                    C.inc(mm, "pe")
                for wi, wsb in enumerate([wqb, wkb, wvb]):
                    for mc in range(2):
                        for ci in range(8):
                            if "noproj" in L1A_SIMPLE:
                                mm = t.matmul(PSf[:, 512 * (2 * wi + mc):512 * (2 * wi + mc + 1)],
                                              identb[:, :], xbj[:, 512 * ci:512 * (ci + 1)],
                                              start=(ci == 0), stop=(ci == 7))
                            else:
                                mm = t.matmul(PSf[:, 512 * (2 * wi + mc):512 * (2 * wi + mc + 1)],
                                              wsb[:, DK * ci + 128 * mc:DK * ci + 128 * (mc + 1)],
                                              xbj[:, 512 * ci:512 * (ci + 1)],
                                              start=(ci == 0), stop=(ci == 7))
                        C.inc(mm, "pe")
                t.wait_ge(C.h["dv"], dv_b0 + DB1 * j + 20)
                for wi, (w0, cd) in enumerate([(q0w, cdq), (k0w, cdk), (v0w, cdv)]):
                    for mc in range(2):
                        for tap in range(4):
                            if "noconv" in L1A_SIMPLE:
                                mm = t.matmul(PSf[:, 512 * (2 * wi + mc):512 * (2 * wi + mc + 1)],
                                              identb[:, :], xbj[:, 0:512],
                                              start=(tap == 0), stop=(tap == 3))
                            else:
                                mm = t.matmul(PSf[:, 512 * (2 * wi + mc):512 * (2 * wi + mc + 1)],
                                              cd[:, 128 * (2 * tap + mc):128 * (2 * tap + mc + 1)],
                                              w0[j % 2][:, 520 * mc + 8 - (0 if "conv8" in L1A_SIMPLE else tap):
                                                 520 * mc + 520 - (0 if "conv8" in L1A_SIMPLE else tap)],
                                              start=(tap == 0), stop=(tap == 3))
                        C.inc(mm, "pe")
                for su in range(4):
                    for ci in range(8):
                        if "nosmalls" in L1A_SIMPLE:
                            mm = t.matmul(PSf[:, 3072 + 8 * su:3072 + 8 * (su + 1)],
                                          xbj[:, 0:128], swb_[:, 0:8],
                                          start=(ci == 0), stop=(ci == 7))
                        else:
                            mm = t.matmul(PSf[:, 3072 + 8 * su:3072 + 8 * (su + 1)],
                                          xbj[:, 512 * ci + 128 * su:512 * ci + 128 * (su + 1)],
                                          swb_[:, 8 * ci:8 * (ci + 1)],
                                          start=(ci == 0), stop=(ci == 7))
                    C.inc(mm, "pe")

        @block.vector
        def _(v):
            for j in range(NJB):
                p0 = pe_b0 + PB1 * j

                def step(op, pe_val=None, ac_val=None):
                    v.wait_ge(C.h["dv"], C.c["dv"])
                    if pe_val is not None:
                        v.wait_ge(C.h["pe"], pe_val)
                    if ac_val is not None:
                        v.wait_ge(C.h["ac"], ac_val)
                    if "bnodve" in L1A_SIMPLE:
                        op = lambda: v.engine_nop()
                    C.inc(op(), "dv")

                if "bnodve1" in L1A_SIMPLE:
                    step(lambda: v.engine_nop(), p0 + 1)
                    step(lambda: v.engine_nop(), p0 + 2)
                else:
                    step(lambda: v.tensor_copy(mb[:, :], PSf[:, 3072:3584]), p0 + 1)
                    step(lambda: v.tensor_copy(rb[:, :], PSf[:, 3072:3584]), p0 + 2)
                for wi, w0 in enumerate([q0w, k0w, v0w]):
                    for mc in range(2):
                        pi = 2 * wi + mc
                        aw = AB1 * (j - 1) + pi + 1 if j > 0 else None
                        if "bnodve2" in L1A_SIMPLE:
                            step(lambda: v.engine_nop(), p0 + 3 + pi, aw)
                            step(lambda: v.engine_nop())
                        else:
                            step(lambda pi=pi: v.scalar_tensor_tensor(
                                tmpf[:, :], mb[:, :], nws[:, pi:pi + 1],
                                PSf[:, 512 * pi:512 * (pi + 1)], ALU.mult, ALU.add),
                                p0 + 3 + pi, aw)
                            step(lambda w0=w0, mc=mc, j=j: v.tensor_mul(
                                w0[j % 2][:, 520 * mc + 8:520 * mc + 520], tmpf[:, :], rb[:, :]))
                for wi, w0 in enumerate([q0w, k0w, v0w]):
                    for mc in range(2):
                        if j == 0 or "nopads" in L1A_SIMPLE:
                            C.inc(v.engine_nop(), "dv")
                        else:
                            step(lambda w0=w0, mc=mc, j=j: v.tensor_copy(
                                w0[j % 2][:, 520 * mc + 4:520 * mc + 8],
                                w0[(j - 1) % 2][:, 520 * mc + 516:520 * mc + 520]))
                for su in range(4):
                    tc = 4 * j + su
                    if "bnodve3" in L1A_SIMPLE:
                        step(lambda: v.engine_nop(), p0 + 15 + su)
                        step(lambda: v.engine_nop())
                    else:
                        step(lambda su=su, tc=tc: v.scalar_tensor_tensor(
                            tmpf[:, 0:8], nws8[:, :], mcold[:, tc:tc + 1],
                            PSf[:, 3072 + 8 * su:3072 + 8 * (su + 1)], ALU.mult, ALU.add),
                            p0 + 18)
                        step(lambda su=su, tc=tc: v.tensor_scalar_mul(
                            smallout[:, 8 * tc:8 * (tc + 1)], tmpf[:, 0:8], rcol[:, tc:tc + 1]))

        @block.scalar
        def _(sc):
            sc.wait_ge(C.h["dv"], dv_b0)
            for j in range(NJB):
                p0 = pe_b0 + PB1 * j
                for pi in range(6):
                    wi, mc = pi // 2, pi % 2
                    sc.wait_ge(C.h["pe"], p0 + 8 + pi + 1)
                    dst = [qT, kT, vT][wi]
                    roww = TP if wi == 2 else T
                    if "bnoact" in L1A_SIMPLE:
                        C.inc(sc.activation(tmpf[:, :], PSf[:, 0:512], AF.Copy), "ac")
                    else:
                        C.inc(sc.activation(
                            dst[:, roww * mc + (32 if wi == 2 else 0) + 512 * j:
                                roww * mc + (32 if wi == 2 else 0) + 512 * (j + 1)],
                            PSf[:, 512 * pi:512 * (pi + 1)], AF.Silu), "ac")

        dv_b1 = C.c["dv"]
        pe_b1 = C.c["pe"]
        ac_b1 = C.c["ac"]
        assert dv_b1 == dv_b0 + DB1 * NJB
        assert pe_b1 == pe_b0 + PB1 * NJB
        assert ac_b1 == ac_invt + AB1 * NJB

        if debug:
            @block.sync
            def _(sync):
                sync.wait_ge(C.h["ac"], ac_b1)
                C.inc(sync.dma_start(out=dbg_ve[:, :], in_=vT[:, :]), "ls", 16)
                C.inc(sync.dma_start(out=dbg_w0[:, 0:1040], in_=v0w[0][:, :]), "ls", 16)
                C.inc(sync.dma_start(out=dbg_w0[:, 1040:2080], in_=v0w[1][:, :]), "ls", 16)
                C.inc(sync.dma_start(out=dbg_mr2[0:1, :], in_=mflat[None, :]), "ls", 16)
                C.inc(sync.dma_start(out=dbg_mr2[1:2, :], in_=rflat[None, :]), "ls", 16)

        if L1A_MAXPHASE < 4:
            return nc
        if "b1tile" in L1A_SIMPLE:
            return nc
        @block.scalar
        def _(sc):
            sc.wait_ge(C.h["dv"], dv_b1)
            for i in range(NT):
                a0 = ac_b1 + 3 * i
                d0 = dv_b1 + 9 * i
                C.inc(sc.activation(betacol[:, i:i + 1], smallout[:, 8 * i:8 * i + 1],
                                    AF.Sigmoid), "ac")
                C.inc(sc.activation(wgcol[:, i:i + 1], smallout[:, 8 * i + 1:8 * i + 2],
                                    AF.Sigmoid, bias=bgb[:, 0:1]), "ac")
                sc.wait_ge(C.h["dv"], d0 + 3)
                C.inc(sc.activation(smallout[:, 8 * i + 2:8 * i + 5],
                                    smallout[:, 8 * i + 2:8 * i + 5], AF.Exp), "ac")

        @block.vector
        def _(v):
            for i in range(NT):
                a0 = ac_b1 + 3 * i

                def step(op, ac_val=None):
                    v.wait_ge(C.h["dv"], C.c["dv"])
                    if ac_val is not None:
                        v.wait_ge(C.h["ac"], ac_val)
                    C.inc(op(), "dv")

                for jj in range(3):
                    step(lambda i=i, jj=jj: v.tensor_scalar(
                        smallout[:, 8 * i + 2 + jj:8 * i + 3 + jj],
                        smallout[:, 8 * i + 2 + jj:8 * i + 3 + jj],
                        blb[:, jj:jj + 1], invtb[:, 0:1], ALU.add, ALU.mult))
                step(lambda i=i: v.tensor_add(s3buf[:, 0:1], smallout[:, 8 * i + 2:8 * i + 3],
                                              smallout[:, 8 * i + 3:8 * i + 4]), a0 + 3)
                step(lambda i=i: v.tensor_add(s3buf[:, 0:1], s3buf[:, 0:1],
                                              smallout[:, 8 * i + 4:8 * i + 5]))
                step(lambda: v.reciprocal(s3buf[:, 1:2], s3buf[:, 0:1]))
                step(lambda i=i: v.tensor_scalar(s3buf[:, 2:3], wgcol[:, i:i + 1], -1.0, 1.0,
                                                 ALU.mult, ALU.add), a0 + 2)
                step(lambda: v.tensor_mul(s3buf[:, 1:2], s3buf[:, 1:2], s3buf[:, 2:3]))
                step(lambda i=i: v.tensor_scalar_mul(wcols[:, 3 * i:3 * (i + 1)],
                                                     smallout[:, 8 * i + 2:8 * i + 5],
                                                     s3buf[:, 1:2]))
        dv_c = C.c["dv"]
        ac_c = C.c["ac"]

        @block.tensor
        def _(t):
            t.wait_ge(C.h["dv"], dv_c)
            C.inc(t.transpose(PSf[0:NT, 0:128], betacol[:, 0:NT], identf[:, :]), "pe")

        pe_btr = C.c["pe"]

        @block.vector
        def _(v):
            v.wait_ge(C.h["pe"], pe_btr)
            C.inc(v.tensor_copy(trm[0:NT, 0:128], PSf[0:NT, 0:128]), "dv")

        dv_btr = C.c["dv"]

        @block.sync
        def _(sync):
            sync.wait_ge(C.h["dv"], dv_btr)
            C.inc(sync.dma_start(out=scr_bt[:, :], in_=trm[0:NT, 0:128]), "ls", 16)

        ls_brow = C.c["ls"]

        if L1A_MAXPHASE < 5:
            return nc
        PEg, DVg, ACg = 28, 32, 5
        pe_e0 = C.c["pe"]
        dv_e0 = C.c["dv"]
        ac_e0 = C.c["ac"]

        def peb(g):
            return pe_e0 + PEg * g

        def dvb(g):
            return dv_e0 + DVg * g

        def acb(g):
            return ac_e0 + ACg * g

        @block.scalar
        def _(sc):
            for g in range(NG):
                if g > 0:
                    sc.wait_ge(C.h["pe"], peb(g - 1) + 2)
                else:
                    sc.wait_ge(C.h["dv"], dv_c)
                for ch in range(2):
                    C.inc(sc.activation(sqb[:, 128 * ch:128 * (ch + 1)],
                                        qT[:, T * ch + 128 * g:T * ch + 128 * (g + 1)],
                                        AF.Square), "ac")
                sc.wait_ge(C.h["pe"], peb(g) + 1)
                for ch in range(2):
                    C.inc(sc.activation(sqb[:, 128 * ch:128 * (ch + 1)],
                                        kT[:, T * ch + 128 * g:T * ch + 128 * (g + 1)],
                                        AF.Square), "ac")
                sc.wait_ge(C.h["pe"], peb(g) + 2)
                C.inc(sc.activation(rn2[0:1, :], PSf[0:1, 2048:2304], AF.Sqrt,
                                    bias=eps6[0:1, 0:1]), "ac")

        @block.tensor
        def _(t):
            for g in range(NG):
                if g == 0:
                    t.wait_ge(C.h["dv"], dv_c)
                    t.wait_ge(C.h["ls"], ls_brow)
                t.wait_ge(C.h["ac"], acb(g) + 2)
                for ch in range(2):
                    mm = t.matmul(PSf[0:1, 2048:2176], onesb[:, 0:1],
                                  sqb[:, 128 * ch:128 * (ch + 1)], start=(ch == 0), stop=(ch == 1))
                C.inc(mm, "pe")
                t.wait_ge(C.h["ac"], acb(g) + 4)
                for ch in range(2):
                    mm = t.matmul(PSf[0:1, 2176:2304], onesb[:, 0:1],
                                  sqb[:, 128 * ch:128 * (ch + 1)], start=(ch == 0), stop=(ch == 1))
                C.inc(mm, "pe")
                t.wait_ge(C.h["dv"], dvb(g) + 1)
                C.inc(t.matmul(PSf[:, 2304:2432], onesf[0:1, 0:128], rn2[0:1, 0:128],
                               start=True, stop=True), "pe")
                C.inc(t.matmul(PSf[:, 2432:2560], onesf[0:1, 0:128], rn2[0:1, 128:256],
                               start=True, stop=True), "pe")
                t.wait_ge(C.h["lba" if g % 2 == 0 else "lbb"], 16 * (g // 2 + 1))
                C.inc(t.matmul(PSf[:, 2560:2688], onesf[0:1, 0:128],
                               browb[g % 2][0:1, :], start=True, stop=True), "pe")
                t.wait_ge(C.h["dv"], dvb(g) + 12)
                for ch in range(2):
                    mm = t.matmul(PSf[:, 0:128], kbg[:, 128 * ch:128 * (ch + 1)],
                                  kT[:, T * ch + 128 * g:T * ch + 128 * (g + 1)],
                                  start=(ch == 0), stop=(ch == 1))
                C.inc(mm, "pe")
                for ch in range(2):
                    mm = t.matmul(PSf[:, 128:256], kT[:, T * ch + 128 * g:T * ch + 128 * (g + 1)],
                                  kbg[:, 128 * ch:128 * (ch + 1)], start=(ch == 0), stop=(ch == 1))
                C.inc(mm, "pe")
                cur, curt = Lb, Ltb
                for lev in range(4):
                    t.wait_ge(C.h["dv"], dvb(g) + 14 + 2 * lev)
                    C.inc(t.matmul(PSf[:, 0:128], curt[:, :], cur[:, :], start=True, stop=True), "pe")
                    C.inc(t.matmul(PSf[:, 128:256], cur[:, :], curt[:, :], start=True, stop=True), "pe")
                    cur, curt = Lk[lev % 2], Lkt[lev]
                t.wait_ge(C.h["dv"], dvb(g) + 22)
                if g > 0:
                    t.wait_ge(C.h["dv"], dvb(g - 1) + 25)
                for ch in range(2):
                    C.inc(t.transpose(PSb[:, 128 * ch:128 * (ch + 1)],
                                      vbg[:, 128 * ch:128 * (ch + 1)], identb[:, :]), "pe")
                for ch in range(2):
                    C.inc(t.transpose(PSb[:, 256 + 128 * ch:256 + 128 * (ch + 1)],
                                      kbg[:, 128 * ch:128 * (ch + 1)], identb[:, :]), "pe")
                for ch in range(2):
                    C.inc(t.transpose(PSb[:, 512 + 128 * ch:512 + 128 * (ch + 1)],
                                      kT[:, T * ch + 128 * g:T * ch + 128 * (g + 1)],
                                      identb[:, :]), "pe")
                for ai in range(4):
                    t.wait_ge(C.h["dv"], dvb(g) + 25 + ai)
                    C.inc(t.matmul(PSf[:, 2048:2560],
                                   Ltb[:, :] if ai == 0 else Lkt[ai - 1][:, :],
                                   uwf[g % 2][:, :], start=True, stop=True), "pe")
                t.wait_ge(C.h["dv"], dvb(g) + 29)
                C.inc(t.matmul(PSf[:, 2048:2560], Lkt[3][:, :], uwf[g % 2][:, :],
                               start=True, stop=True), "pe")
                t.wait_ge(C.h["dv"], dvb(g) + 31)
                if g > 0:
                    t.wait_ge(C.h["dv"], dvb(g - 1) + 32)
                for ch in range(2):
                    C.inc(t.transpose(PSb[:, 768 + 128 * ch:768 + 128 * (ch + 1)],
                                      uwb[g % 2][:, 256 + 128 * ch:256 + 128 * (ch + 1)],
                                      identb[:, :]), "pe")

        @block.vector
        def _(v):
            for g in range(NG):
                def step(op, pe_val=None, sem=None, semval=None):
                    v.wait_ge(C.h["dv"], C.c["dv"])
                    if pe_val is not None:
                        v.wait_ge(C.h["pe"], pe_val)
                    if sem is not None:
                        v.wait_ge(C.h[sem], semval)
                    C.inc(op(), "dv")

                step(lambda: v.reciprocal(rn2[0:1, :], rn2[0:1, :]),
                     sem="ac", semval=acb(g) + 5)
                step(lambda: v.tensor_copy(qrb[:, :], PSf[:, 2304:2432]), peb(g) + 4)
                step(lambda: v.tensor_copy(krb[:, :], PSf[:, 2432:2560]), peb(g) + 4)
                step(lambda: v.tensor_copy(bb[:, :], PSf[:, 2560:2688]), peb(g) + 5)
                pw = peb(g - 1) + 21 if g > 0 else None
                for ch in range(2):
                    step(lambda ch=ch: v.tensor_mul(
                        qT[:, T * ch + 128 * g:T * ch + 128 * (g + 1)],
                        qT[:, T * ch + 128 * g:T * ch + 128 * (g + 1)], qrb[:, :]), pw)
                for ch in range(2):
                    step(lambda ch=ch: v.tensor_mul(
                        kT[:, T * ch + 128 * g:T * ch + 128 * (g + 1)],
                        kT[:, T * ch + 128 * g:T * ch + 128 * (g + 1)], krb[:, :]), pw)
                for ch in range(2):
                    step(lambda ch=ch: v.tensor_mul(
                        kbg[:, 128 * ch:128 * (ch + 1)],
                        kT[:, T * ch + 128 * g:T * ch + 128 * (g + 1)], bb[:, :]), pw)
                for ch in range(2):
                    step(lambda ch=ch: v.tensor_mul(
                        vbg[:, 128 * ch:128 * (ch + 1)],
                        vT[:, TP * ch + 32 + 128 * g:TP * ch + 32 + 128 * (g + 1)], bb[:, :]), pw)
                step(lambda: v.tensor_mul(Lb[:, :], PSf[:, 0:128], mbLs[:, :]),
                     peb(g) + 7)
                step(lambda: v.tensor_mul(Ltb[:, :], PSf[:, 128:256], mbUs[:, :]),
                     peb(g) + 7)
                for lev in range(4):
                    step(lambda lev=lev: v.tensor_copy(Lk[lev % 2][:, :], PSf[:, 0:128]),
                         peb(g) + 9 + 2 * lev)
                    step(lambda lev=lev: v.tensor_copy(Lkt[lev][:, :], PSf[:, 128:256]),
                         peb(g) + 9 + 2 * lev)
                step(lambda g=g: v.tensor_copy(uwf[g % 2][:, 0:256], PSb[:, 0:256]),
                     peb(g) + 21)
                step(lambda g=g: v.tensor_copy(uwf[g % 2][:, 256:512], PSb[:, 256:512]),
                     peb(g) + 21)
                step(lambda g=g: v.tensor_copy(kntb[g % 2][:, :], PSb[:, 512:768]),
                     peb(g) + 21, "ska" if g % 2 == 0 else "skb", 16 * (g // 2))
                step(lambda g=g: v.tensor_sub(uwf[g % 2][:, :], uwf[g % 2][:, :],
                                              PSf[:, 2048:2560]), peb(g) + 22)
                for ai in range(4):
                    step(lambda g=g: v.tensor_add(uwf[g % 2][:, :], uwf[g % 2][:, :],
                                                  PSf[:, 2048:2560]), peb(g) + 23 + ai)
                step(lambda g=g: v.tensor_copy(uwb[g % 2][:, :], uwf[g % 2][:, :]),
                     None, "sua" if g % 2 == 0 else "sub", 16 * (g // 2))
                step(lambda g=g: v.tensor_copy(wtb[g % 2][:, :], PSb[:, 768:1024]),
                     peb(g) + 28, "swa" if g % 2 == 0 else "swb", 16 * (g // 2))

        @block.sync
        def _(sync):
            sync.wait_ge(C.h["ls"], ls_brow)
            for g in range(NG):
                if g >= 2:
                    sync.wait_ge(C.h["pe"], peb(g - 2) + 5)
                C.inc(sync.dma_start(out=browb[g % 2][0:1, :], in_=scr_bt[g:g + 1, :]),
                      "lba" if g % 2 == 0 else "lbb", 16)
                sync.wait_ge(C.h["dv"], dvb(g) + 25)
                C.inc(sync.dma_start(out=scr_knt[128 * g:128 * (g + 1), :],
                                     in_=kntb[g % 2][:, :]), "ska" if g % 2 == 0 else "skb", 16)
                sync.wait_ge(C.h["dv"], dvb(g) + 31)
                C.inc(sync.dma_start(out=scr_uw[128 * g:128 * (g + 1), :],
                                     in_=uwb[g % 2][:, :]), "sua" if g % 2 == 0 else "sub", 16)
                sync.wait_ge(C.h["dv"], dvb(g) + 32)
                C.inc(sync.dma_start(out=scr_wt[128 * g:128 * (g + 1), :],
                                     in_=wtb[g % 2][:, :]), "swa" if g % 2 == 0 else "swb", 16)

        pe_e1 = C.c["pe"]
        dv_e1 = C.c["dv"]
        ac_e1 = C.c["ac"]
        assert pe_e1 == pe_e0 + PEg * NG and dv_e1 == dv_e0 + DVg * NG and ac_e1 == ac_e0 + ACg * NG

        if L1A_MAXPHASE < 6:
            return nc
        @block.vector
        def _(v):
            v.wait_ge(C.h["dv"], dv_e1)
            v.wait_ge(C.h["lw"], lw_all)
            C.inc(v.tensor_copy(Smir[:, :], Sf32[:, :]), "dv")

        dv_s0 = C.c["dv"]
        pe_s0 = C.c["pe"]
        ac_s0 = C.c["ac"]

        @block.sync
        def _(sync):
            for i in range(NCH):
                g2 = i // 4
                if i % 4 == 0:
                    sync.wait_ge(C.h["swa" if g2 % 2 == 0 else "swb"], 16 * (g2 // 2 + 1))
                    if g2 >= 2:
                        sync.wait_ge(C.h["pe"], pe_s0 + 5 * (4 * (g2 - 1)) + 1)
                    C.inc(sync.dma_start(out=wtl[g2 % 2][:, :],
                                         in_=scr_wt[128 * g2:128 * (g2 + 1), :]),
                          "lwa" if g2 % 2 == 0 else "lwb", 16)
                sync.wait_ge(C.h["sua" if g2 % 2 == 0 else "sub"], 16 * (g2 // 2 + 1))
                sync.wait_ge(C.h["ska" if g2 % 2 == 0 else "skb"], 16 * (g2 // 2 + 1))
                if i >= 2:
                    sync.wait_ge(C.h["dv"], dv_s0 + 4 * (i - 2) + 1)
                    sync.wait_ge(C.h["pe"], pe_s0 + 5 * (i - 2) + 5)
                C.inc(sync.dma_start(out=uwl[i % 2][0:32, :],
                                     in_=scr_uw[32 * i:32 * (i + 1), :]),
                      "lua" if i % 2 == 0 else "lub", 16)
                C.inc(sync.dma_start(out=kntl[i % 2][0:32, :],
                                     in_=scr_knt[32 * i:32 * (i + 1), :]),
                      "lka" if i % 2 == 0 else "lkb", 16)
                if i >= 1:
                    sync.wait_ge(C.h["ac"], ac_s0 + 2 * (i - 1) + 1)
                    C.inc(sync.dma_start(out=scr_del[32 * (i - 1):32 * i, :],
                                         in_=otmp[0:32, :] if (i - 1) % 2 == 0 else otm2[0:32, :]),
                          "sdla" if (i - 1) % 2 == 0 else "sdlb", 16)
            sync.wait_ge(C.h["ac"], ac_s0 + 2 * (NCH - 1) + 1)
            C.inc(sync.dma_start(out=scr_del[32 * (NCH - 1):32 * NCH, :],
                                 in_=otmp[0:32, :] if (NCH - 1) % 2 == 0 else otm2[0:32, :]),
                  "sdla" if (NCH - 1) % 2 == 0 else "sdlb", 16)

        @block.tensor
        def _(t):
            for i in range(NCH):
                g2 = i // 4
                q_ = i % 4
                p0 = pe_s0 + 5 * i
                t.wait_ge(C.h["lwa" if g2 % 2 == 0 else "lwb"], 16 * (g2 // 2 + 1))
                if i > 0:
                    t.wait_ge(C.h["ac"], ac_s0 + 2 * (i - 1) + 2)
                    t.wait_ge(C.h["dv"], dv_s0 + 4 * (i - 1) + 1)
                else:
                    t.wait_ge(C.h["dv"], dv_s0)
                for ch in range(2):
                    mm = t.matmul(PSf[0:32, 1536:1792],
                                  wtl[g2 % 2][:, 128 * ch + 32 * q_:128 * ch + 32 * (q_ + 1)],
                                  Smir[:, 256 * ch:256 * (ch + 1)],
                                  start=(ch == 0), stop=(ch == 1))
                C.inc(mm, "pe")
                if i > 0:
                    t.wait_ge(C.h["dv"], dv_s0 + 4 * (i - 1) + 2)
                for ch in range(2):
                    mm = t.matmul(PSf[0:32, 1024:1056],
                                  kT[:, T * ch + 32 * i:T * ch + 32 * (i + 1)],
                                  qT[:, T * ch + 32 * i:T * ch + 32 * (i + 1)],
                                  start=(ch == 0), stop=(ch == 1))
                C.inc(mm, "pe")
                if i > 0:
                    t.wait_ge(C.h["ac"], ac_s0 + 2 * (i - 1) + 1)
                for ch in range(2):
                    mm = t.matmul(PSf[0:32, 2048:2304],
                                  qT[:, T * ch + 32 * i:T * ch + 32 * (i + 1)],
                                  Smir[:, 256 * ch:256 * (ch + 1)],
                                  start=(ch == 0), stop=False)
                C.inc(mm, "pe")
                t.wait_ge(C.h["dv"], dv_s0 + 4 * i + 2)
                C.inc(t.matmul(PSf[0:32, 2048:2304], atb_[0:32, :], upb[0:32, :],
                               start=False, stop=True), "pe")
                t.wait_ge(C.h["lka" if i % 2 == 0 else "lkb"], 16 * (i // 2 + 1))
                if i > 0:
                    t.wait_ge(C.h["dv"], dv_s0 + 4 * (i - 1) + 4)
                for ch in range(2):
                    mm = t.matmul(SB0 if ch == 0 else SB1,
                                  kntl[i % 2][0:32, 128 * ch:128 * (ch + 1)],
                                  upb[0:32, :],
                                  start=True, stop=True)
                C.inc(mm, "pe")

        @block.vector
        def _(v):
            for i in range(NCH):
                p0 = pe_s0 + 5 * i
                v.wait_ge(C.h["dv"], C.c["dv"])
                v.wait_ge(C.h["pe"], p0 + 1)
                v.wait_ge(C.h["lua" if i % 2 == 0 else "lub"], 16 * (i // 2 + 1))
                C.inc(v.tensor_sub(upb[0:32, :], uwl[i % 2][0:32, 0:256],
                                   PSf[0:32, 1536:1792]), "dv")
                v.wait_ge(C.h["dv"], C.c["dv"])
                v.wait_ge(C.h["pe"], p0 + 2)
                C.inc(v.tensor_mul(atb_[0:32, :], PSf[0:32, 1024:1056], mtu[0:32, :]), "dv")
                v.wait_ge(C.h["dv"], C.c["dv"])
                v.wait_ge(C.h["pe"], p0 + 5)
                C.inc(v.tensor_add(Sf32[:, 0:256], Sf32[:, 0:256], SB0), "dv")
                v.wait_ge(C.h["dv"], C.c["dv"])
                C.inc(v.tensor_add(Sf32[:, 256:512], Sf32[:, 256:512], SB1), "dv")

        @block.scalar
        def _(sc):
            for i in range(NCH):
                p0 = pe_s0 + 5 * i
                sc.wait_ge(C.h["pe"], p0 + 4)
                if i >= 2:
                    sc.wait_ge(C.h["sdla" if i % 2 == 0 else "sdlb"], 16 * (i // 2))
                dst = otmp if i % 2 == 0 else otm2
                C.inc(sc.activation(dst[0:32, :], PSf[0:32, 2048:2304], AF.Copy), "ac")
                sc.wait_ge(C.h["dv"], dv_s0 + 4 * i + 4)
                C.inc(sc.activation(Smir[:, :], Sf32[:, :], AF.Copy), "ac")

        pe_s1 = C.c["pe"]
        dv_s1 = C.c["dv"]
        ac_s1 = C.c["ac"]
        sdla_all = C.c["sdla"]
        sdlb_all = C.c["sdlb"]

        if L1A_MAXPHASE < 7:
            return nc
        PEm, ACm, DVm = 28, 12, 44

        def mpe(j):
            return pe_s1 + PEm * j

        def mac(j):
            return ac_s1 + ACm * j

        def mdv(j):
            return dv_s1 + DVm * j

        @block.sync
        def _(sync):
            sync.wait_ge(C.h["sdla"], sdla_all)
            sync.wait_ge(C.h["sdlb"], sdlb_all)
            for j in range(NJ):
                for su in range(4):
                    u = 4 * j + su
                    if u >= 2:
                        sync.wait_ge(C.h["dv"], mdv((u - 2) // 4) + 11 * ((u - 2) % 4 + 1))
                    C.inc(sync.dma_start(out=dtok[u % 2][:, :],
                                         in_=scr_del[128 * u:128 * (u + 1), :]),
                          "dla" if u % 2 == 0 else "dlb", 16)
                    sync.wait_ge(C.h["dv"], mdv(j) + 11 * (su + 1))
                    C.inc(sync.dma_start(out=o_out[128 * u:128 * (u + 1), :], in_=obf[:, :]),
                          "soo", 16)

        @block.tensor
        def _(t):
            for j in range(NJ):
                for fi, (nt, fb, base) in enumerate([(3, fsb, 0), (31, flb, 1024)]):
                    for mc in range(2):
                        if j > 0:
                            t.wait_ge(C.h["ac"], mac(j - 1) + 2 * fi + mc + 1)
                        else:
                            t.wait_ge(C.h["dv"], dv_s1)
                        for tap in range(nt):
                            mm = t.matmul(PSf[:, base + 512 * mc:base + 512 * (mc + 1)],
                                          fb[:, 128 * (2 * tap + mc):128 * (2 * tap + mc + 1)],
                                          vT[:, TP * mc + 32 + 512 * j - tap:
                                             TP * mc + 32 + 512 * (j + 1) - tap],
                                          start=(tap == 0), stop=(tap == nt - 1))
                        C.inc(mm, "pe")
                for su in range(4):
                    u = 4 * j + su
                    t.wait_ge(C.h["ac"], mac(j) + 4)
                    if u > 0:
                        t.wait_ge(C.h["dv"], mdv((u - 1) // 4) + 11 * ((u - 1) % 4) + 3)
                    for ch in range(2):
                        C.inc(t.transpose(PSb[:, 128 * ch:128 * (ch + 1)],
                                          vT[:, TP * ch + 32 + 128 * u:TP * ch + 32 + 128 * (u + 1)],
                                          identb[:, :]), "pe")
                    for ch in range(2):
                        C.inc(t.transpose(PSb[:, 256 + 128 * ch:256 + 128 * (ch + 1)],
                                          sTb[j % 2][:, 512 * ch + 128 * su:512 * ch + 128 * (su + 1)],
                                          identb[:, :]), "pe")
                    for ch in range(2):
                        C.inc(t.transpose(PSb[:, 512 + 128 * ch:512 + 128 * (ch + 1)],
                                          lTb[j % 2][:, 512 * ch + 128 * su:512 * ch + 128 * (su + 1)],
                                          identb[:, :]), "pe")

        @block.scalar
        def _(sc):
            for j in range(NJ):
                for fi in range(2):
                    for mc in range(2):
                        sc.wait_ge(C.h["pe"], mpe(j) + 2 * fi + mc + 1)
                        if j >= 2:
                            sc.wait_ge(C.h["pe"], mpe(j - 1) + 28)
                        base = 1024 * fi
                        dstb = sTb if fi == 0 else lTb
                        C.inc(sc.activation(dstb[j % 2][:, 512 * mc:512 * (mc + 1)],
                                            PSf[:, base + 512 * mc:base + 512 * (mc + 1)],
                                            AF.Copy), "ac")
                for su in range(4):
                    sc.wait_ge(C.h["dv"], mdv(j) + 11 * su + 7)
                    C.inc(sc.activation(osq[:, :], of_[:, :], AF.Square), "ac")
                    sc.wait_ge(C.h["dv"], mdv(j) + 11 * su + 8)
                    C.inc(sc.activation(orr[:, 1:2], orr[:, 0:1], AF.Sqrt,
                                        bias=epsb[:, 0:1], scale=1.0 / DK), "ac")

        @block.vector
        def _(v):
            for j in range(NJ):
                for su in range(4):
                    u = 4 * j + su

                    def step(op, pe_val=None, ac_val=None, sem=None, semval=None):
                        v.wait_ge(C.h["dv"], C.c["dv"])
                        if pe_val is not None:
                            v.wait_ge(C.h["pe"], pe_val)
                        if ac_val is not None:
                            v.wait_ge(C.h["ac"], ac_val)
                        if sem is not None:
                            v.wait_ge(C.h[sem], semval)
                        C.inc(op(), "dv")

                    tb = mpe(j) + 4 + 6 * su
                    step(lambda: v.tensor_copy(vtok[:, :], PSb[:, 0:256]), tb + 6)
                    step(lambda: v.tensor_copy(stok[:, :], PSb[:, 256:512]), tb + 6)
                    step(lambda: v.tensor_copy(ltok[:, :], PSb[:, 512:768]), tb + 6)
                    step(lambda u=u: v.tensor_scalar_mul(mixf[:, :], vtok[:, :],
                                                         wcols[:, 3 * u:3 * u + 1]))
                    step(lambda u=u: v.scalar_tensor_tensor(mixf[:, :], stok[:, :],
                                                            wcols[:, 3 * u + 1:3 * u + 2],
                                                            mixf[:, :], ALU.mult, ALU.add))
                    step(lambda u=u: v.scalar_tensor_tensor(mixf[:, :], ltok[:, :],
                                                            wcols[:, 3 * u + 2:3 * u + 3],
                                                            mixf[:, :], ALU.mult, ALU.add))
                    step(lambda u=u: v.scalar_tensor_tensor(of_[:, :], dtok[u % 2][:, :],
                                                            wgcol[:, u:u + 1],
                                                            mixf[:, :], ALU.mult, ALU.add),
                         sem="dla" if u % 2 == 0 else "dlb", semval=16 * (u // 2 + 1))
                    step(lambda: v.reduce_sum(orr[:, 0:1], osq[:, :], axis=AX.X),
                         ac_val=mac(j) + 4 + 2 * su + 1)
                    step(lambda: v.reciprocal(orr[:, 1:2], orr[:, 1:2]),
                         ac_val=mac(j) + 4 + 2 * su + 2)
                    step(lambda: v.tensor_scalar_mul(of_[:, :], of_[:, :], orr[:, 1:2]),
                         sem="soo", semval=16 * u)
                    step(lambda: v.tensor_mul(obf[:, :], of_[:, :], rmsb[:, :]))

        @block.sync
        def _(sync):
            C.wait(sync, "soo")
            sync.wait_ge(C.h["dv"], dv_s1)
            sync.wait_ge(C.h["ac"], ac_b1)
            C.inc(sync.dma_start(out=sfin_out[:, :], in_=Sf32[:, :]), "sst", 16)
            for w0, dst in [(q0w, ctqo_out), (k0w, ctko_out), (v0w, ctvo_out)]:
                for mc in range(2):
                    C.inc(sync.dma_start(out=dst[:, 4 * mc:4 * (mc + 1)],
                                         in_=w0[(NJ - 1) % 2][:, 520 * mc + 516:520 * mc + 520]),
                          "sst", 16)
            for ch in range(2):
                C.inc(sync.dma_start(out=vtlo_out[:, 32 * ch:32 * (ch + 1)],
                                     in_=vT[:, TP * ch + T:TP * ch + T + 32]), "sst", 16)
            C.wait(sync, "sst")
            if debug:
                C.inc(sync.dma_start(out=dbg["dbg_q"][:, :], in_=qT[:, :]), "soo", 16)
                C.inc(sync.dma_start(out=dbg["dbg_k"][:, :], in_=kT[:, :]), "soo", 16)
                C.inc(sync.dma_start(out=dbg["dbg_v"][:, :], in_=vT[:, :]), "soo", 16)
                C.inc(sync.dma_start(out=dbg["dbg_uw"][:, :], in_=scr_uw[:, :]), "soo", 16)
                C.inc(sync.dma_start(out=dbg["dbg_del"][:, :], in_=scr_del[:, :]), "soo", 16)
                C.inc(sync.dma_start(out=dbg_lv[:, 0:128], in_=Lb[:, :]), "soo", 16)
                C.inc(sync.dma_start(out=dbg_lv[:, 128:256], in_=Ltb[:, :]), "soo", 16)
                for lv_ in range(4):
                    C.inc(sync.dma_start(out=dbg_lv[:, 256 + 128 * lv_:384 + 128 * lv_],
                                         in_=Lkt[lv_][:, :]), "soo", 16)
                C.inc(sync.dma_start(out=dbg_kv[:, 0:256], in_=kbg[:, :]), "soo", 16)
                C.inc(sync.dma_start(out=dbg_kv[:, 256:512], in_=vbg[:, :]), "soo", 16)
                C.inc(sync.dma_start(out=dbg_kv[:, 512:768], in_=kntb[(NG - 1) % 2][:, :]), "soo", 16)
                C.inc(sync.dma_start(out=dbg_sm[:, 0:8 * min(NT, 64)],
                                 in_=smallout[:, 0:8 * min(NT, 64)]), "soo", 16)
                C.wait(sync, "soo")

    return nc



def _diag_tiles(w, taps):
    out = np.zeros((128, taps * 2 * 128), np.float32)
    for tap in range(taps):
        for mc in range(2):
            blk = np.diag(w[128 * mc:128 * (mc + 1), taps - 1 - tap])
            out[:, 128 * (2 * tap + mc):128 * (2 * tap + mc + 1)] = blk
    return out


def l1a_masks():
    mbL = np.zeros((128, 128), np.float32)
    for b in range(4):
        for r in range(32):
            for c in range(r):
                mbL[32 * b + r, 32 * b + c] = 1.0
    mtu = np.triu(np.ones((32, 32), np.float32))
    return {"mbLf": mbL, "mbUf": mbL.T.copy(),
            "mtu32": mtu.astype(NP_BF16)}


def l1a_inputs(T, inputs, x2_full_by_batch, state=None):
    wq = np.asarray(inputs["L1_attn_Wq"], np.float32)
    wk = np.asarray(inputs["L1_attn_Wk"], np.float32)
    wv = np.asarray(inputs["L1_attn_Wv"], np.float32)
    cq = np.asarray(inputs["L1_attn_conv_q"], np.float32)
    ck = np.asarray(inputs["L1_attn_conv_k"], np.float32)
    cv = np.asarray(inputs["L1_attn_conv_v"], np.float32)
    wb = np.asarray(inputs["L1_attn_Wb"], np.float32)
    wg = np.asarray(inputs["L1_attn_Wg"], np.float32)
    wl = np.asarray(inputs["L1_attn_Wl"], np.float32)
    bg = np.asarray(inputs["L1_attn_bg"], np.float32)
    bl = np.asarray(inputs["L1_attn_bl"], np.float32)
    ltemp = np.asarray(inputs["L1_attn_log_temp"], np.float32)
    fs = np.asarray(inputs["L1_attn_fir_short"], np.float32)
    fl = np.asarray(inputs["L1_attn_fir_long"], np.float32)
    rms = np.asarray(inputs["L1_attn_rms_w"], np.float32)
    cc = const_inputs()
    msk = l1a_masks()
    maps = []
    for c in range(NCORE):
        b, h = c // 4, c % 4
        sl = slice(256 * h, 256 * (h + 1))
        smallw = np.zeros((D, 8), np.float32)
        smallw[:, 0] = wb[:, h]
        smallw[:, 1] = wg[:, h]
        smallw[:, 2:5] = wl[:, 3 * h:3 * h + 3]
        m = {"x": x2_full_by_batch[b],
             "xT": np.ascontiguousarray(x2_full_by_batch[b].T),
             "wq": np.ascontiguousarray(wq[:, sl]),
             "wk": np.ascontiguousarray(wk[:, sl]),
             "wv": np.ascontiguousarray(wv[:, sl]),
             "cqd": _diag_tiles(cq[sl], 4), "ckd": _diag_tiles(ck[sl], 4),
             "cvd": _diag_tiles(cv[sl], 4),
             "fsd": _diag_tiles(fs[h].reshape(256, 3), 3),
             "fld": _diag_tiles(fl[h].reshape(256, 31), 31),
             "smallw": smallw,
             "bgs": bg[h].reshape(1, 1), "blrow": bl[3 * h:3 * h + 3].reshape(1, 3),
             "ltemp": ltemp[h].reshape(1, 1), "rmsrow": rms.reshape(1, 256)}
        if state is None:
            m["sinit"] = np.zeros((128, 512), np.float32)
            m["ctq"] = np.zeros((128, 8), NP_BF16)
            m["ctk"] = np.zeros((128, 8), NP_BF16)
            m["ctv"] = np.zeros((128, 8), NP_BF16)
            m["vtl"] = np.zeros((128, 64), NP_BF16)
        else:
            m["sinit"] = state[c]["sinit"]
            m["ctq"] = state[c]["ctq"]
            m["ctk"] = state[c]["ctk"]
            m["ctv"] = state[c]["ctv"]
            m["vtl"] = state[c]["vtl"]
        m.update(cc)
        m.update(msk)
        maps.append(m)
    return maps
